# version 10
# baseline (speedup 1.0000x reference)
import numpy as np
import ml_dtypes
import concourse.bass as bass
import concourse.mybir as mybir
from concourse.bass_utils import run_bass_kernel_spmd

F32 = mybir.dt.float32
BF16 = mybir.dt.bfloat16
AF = mybir.ActivationFunctionType
ALU = mybir.AluOpType
AXX = mybir.AxisListType.X

S = 4096
HID = 4096
D = 128
NH = 8
FS = 1024
SCALE = 0.08838834764831845
C23 = float(np.float32(1.5 * 2.0 ** 23))
C52 = float(np.float32(1.5 * 2.0 ** 52))
TEXT_LO, TEXT_HI = 513, 4040
DBG_NTC = 8
DBG_NH = 8
DBG_NORMS = True
DBG_QBAR = True
DBG_ROPE = True


class B:
    ENGS = ("sync", "gpsimd", "vector", "scalar", "tensor")

    def __init__(self, nc):
        self.nc = nc
        self.prog = {e: [] for e in self.ENGS}
        self.sems = {}
        self.ctr = {}
        self.rel = {}
        self.nbar = 0

    def sem(self, name):
        if name not in self.sems:
            self.sems[name] = self.nc.alloc_semaphore(name=name)
            self.ctr[name] = 0
        return self.sems[name]

    def op(self, eng, meth, *args, inc=None, **kw):
        s = self.sems[inc[0]] if inc else None
        if inc:
            self.ctr[inc[0]] += inc[1]

        def fn(e, meth=meth, args=args, kw=kw, s=s, iv=inc[1] if inc else None):
            ins = getattr(e, meth)(*args, **kw)
            if s is not None:
                ins.then_inc(s, iv)
            return ins
        self.prog[eng].append(fn)
        return self.ctr[inc[0]] if inc else None

    def opw(self, eng, meth, *args, **kw):
        sem = {"vector": "vV", "scalar": "aA", "tensor": "pP",
               "gpsimd": "gG"}[eng]
        v = self.op(eng, meth, *args, inc=(sem, 1), **kw)
        self.wait(eng, sem, v)
        return v

    def wait(self, eng, semname, val):
        if val is None or val <= 0:
            return
        s = self.sems[semname]
        self.prog[eng].append(lambda e, s=s, v=val: e.wait_ge(s, v))

    def dma(self, eng, semname, out, in_, **kw):
        self.sem(semname)
        return self.op(eng, "dma_start", out, in_, inc=(semname, 16), **kw)

    def released(self, key, semname, val):
        self.rel.setdefault(key, []).append((semname, val))

    def need(self, eng, key):
        for (s, v) in self.rel.pop(key, []):
            self.wait(eng, s, v)

    def run_block(self):
        self.sem("bar")
        self.nbar += 1
        for e in self.ENGS:
            if e != "sync":
                self.prog[e].append(lambda E: E.drain())
            s = self.sems["bar"]
            self.prog[e].append(lambda E, s=s: E.sem_inc(s, 1))
        nc = self.nc
        with nc.Block() as block:
            for ename, reg in (("sync", block.sync), ("gpsimd", block.gpsimd),
                               ("vector", block.vector), ("scalar", block.scalar),
                               ("tensor", block.tensor)):
                lst = self.prog[ename]

                def body(E, lst=lst):
                    for fn in lst:
                        fn(E)
                reg(body)
        self.prog = {e: [] for e in self.ENGS}

    def start_block_waits(self):
        for e in self.ENGS:
            self.wait(e, "bar", 5 * self.nbar)


def build(upto=4, sub=None):
    nc = bass.Bass("TRN2", target_bir_lowering=False)

    x_b = nc.dram_tensor("x_b", [S, HID], F32, kind="ExternalInput")
    xres_i = nc.dram_tensor("xres_i", [S, FS], F32, kind="ExternalInput")
    qwT = nc.dram_tensor("qwT", [HID, FS], F32, kind="ExternalInput")
    kwT = nc.dram_tensor("kwT", [HID, FS], F32, kind="ExternalInput")
    vwT = nc.dram_tensor("vwT", [HID, FS], F32, kind="ExternalInput")
    owT = nc.dram_tensor("owT", [HID, FS], F32, kind="ExternalInput")
    fw_nat = nc.dram_tensor("fw_nat", [FS, HID], F32, kind="ExternalInput")
    fbias = nc.dram_tensor("fbias", [1, FS], F32, kind="ExternalInput")
    cosT = nc.dram_tensor("cosT", [D, S], F32, kind="ExternalInput")
    sinT = nc.dram_tensor("sinT", [D, S], F32, kind="ExternalInput")
    masks = nc.dram_tensor("masks", [4, 128, 512], BF16, kind="ExternalInput")
    flag = nc.dram_tensor("flag", [1, 1], F32, kind="ExternalInput")
    y = nc.dram_tensor("y", [S, FS], F32, kind="ExternalOutput")

    xn_tok = nc.dram_tensor("xn_tok", [S, HID], BF16)
    xn_feat = nc.dram_tensor("xn_feat", [HID, S], BF16)
    qbuf = nc.dram_tensor("qbuf", [NH * D, S], BF16)
    kbuf = nc.dram_tensor("kbuf", [NH * D, S], BF16)
    vtok = nc.dram_tensor("vtok", [NH, 128, 32, D], BF16)
    att_mine = nc.dram_tensor("att_mine", [NH * D, S], BF16)
    att_full = nc.dram_tensor("att_full", [HID, S], BF16)
    wq_nat = nc.dram_tensor("wq_nat", [FS, HID], BF16)
    imp_in = nc.dram_tensor("imp_in", [1, 512], F32)
    imp_out = nc.dram_tensor("imp_out", [1, 512], F32)

    b = B(nc)
    A = nc.alloc_sbuf_tensor
    P = nc.alloc_psum_tensor

    def _cut1():
        dy = A("dummy_y2", [128, 512], F32)
        nvd = b.op("vector", "memset", dy[:], 0.0, inc=("vV", 1))
        b.wait("sync", "vV", nvd)
        b.dma("sync", "dY0", y[0:128, 0:512], dy[:])
        b.wait("sync", "dY0", b.ctr["dY0"])
        b.run_block()
        nc.finalize()
        return nc

    def _cut(k):
        if upto != k:
            return False
        dy = A("dummy_y", [128, 512], F32)
        nvd = b.op("vector", "memset", dy[:], 0.0, inc=("vV", 1))
        b.wait("sync", "vV", nvd)
        b.dma("sync", "dY0", y[0:128, 0:512], dy[:])
        b.wait("sync", "dY0", b.ctr["dY0"])
        b.run_block()
        nc.finalize()
        return True

    keepfull = A("keepfull", [128, 6], BF16)
    keepf32 = A("keepf32", [128, 6], F32)
    maxq2 = A("maxq2", [1, NH], F32)
    maxk2 = A("maxk2", [1, NH], F32)
    sbP, psP = nc.sbuf_base, nc.psum_base

    for s in (["dX0", "dX1", "dXnT0", "dXnT1", "dFlag"]
              + [f"dXtr{i}" for i in range(5)] + [f"dFeat{i}" for i in range(5)]
              + [
              "dCast", "dW", "dCS0", "dCS1", "dXC0", "dXC1", "dSp0", "dSp1", "dImp",
              "cC", "dKeep", "dQH0", "dQH1", "dKH0", "dKH1", "dVH0", "dVH1",
              "dAtt0", "dAtt1", "dWn0", "dWn1", "dWqn0", "dWqn1", "dWqT", "dOsb",
              "dAC0", "dAC1", "dXn30", "dXn31", "dXr0", "dXr1", "dY0", "dY1",
              "dBias", "vV", "aA", "pP", "gG", "bar"]):
        b.sem(s)

    x_sb = [A("x_sb0", [128, HID], F32), A("x_sb1", [128, HID], F32)]
    xn_sb = [A("xn_sb0", [128, HID], BF16), A("xn_sb1", [128, HID], BF16)]
    stt = [A("stt0", [128, 48], F32), A("stt1", [128, 48], F32)]
    agb = [A("ag0", [128, 2], F32), A("ag1", [128, 2], F32)]
    negmean = [A("negmean0", [128, 1], F32), A("negmean1", [128, 1], F32)]
    lnv = [A("lnv0", [128, 1], F32), A("lnv1", [128, 1], F32)]
    rstd = [A("rstd0", [128, 1], F32), A("rstd1", [128, 1], F32)]
    xtr = [A(f"xtr{i}", [128, 1024], BF16) for i in range(5)]
    epscol = A("epscol", [128, 1], F32)

    wsb = A("wsb", [128, 32 * 1024], BF16)
    xnc = [A("xnc0", [128, 32 * 512], BF16), A("xnc1", [128, 32 * 512], BF16)]
    coss = [A("coss0", [128, 512], F32)] * 2
    sins = [A("sins0", [128, 512], F32)] * 2
    rope1 = A("rope1", [128, 512], F32)
    rope2 = A("rope2", [128, 512], F32)
    rt = A("rt", [128, 512], F32)
    qout = [A("qout0", [128, 512], BF16), A("qout1", [128, 512], BF16)]
    sqn = [A("sqn0", [128, 512], BF16), A("sqn1", [128, 512], BF16),
           A("sqn2", [128, 512], BF16)]
    mx = A("mx", [1, 1], F32)
    qtmp = A("qtmp", [128, 1], F32)
    qbar = A("qbar", [128, NH], F32)
    qbar_bf = A("qbar_bf", [128, NH], BF16)
    ones128bf = A("ones128bf", [128, 1], BF16)
    onesrowA = A("onesrowA", [1, 128], F32)
    flag_sb = A("flag_sb", [1, 1], F32)
    imp_sb = A("imp_sb", [1, 512], F32)
    imp2_sb = A("imp2_sb", [1, 512], F32)
    impshift = A("impshift", [128, 5], F32)
    Bsb = rope2
    cmpscr = rope1
    rankc = A("rankc", [128, 1], F32)
    r512 = A("r512", [1, 1], F32)
    kvis = A("kvis", [128, 512], BF16)
    PRJ = [P("PRJ0", [128, 512], F32), P("PRJ1", [128, 512], F32),
           P("PRJ2", [128, 512], F32)]
    R1 = P("R1", [1, 512], F32)
    R2 = P("R2", [1, 512], F32)

    b.op("vector", "memset", ones128bf[:], 1.0)
    b.op("vector", "memset", onesrowA[:], 1.0)
    b.op("vector", "memset", qbar[:], 0.0)
    b.op("vector", "memset", impshift[:], 0.0)
    b.op("vector", "memset", maxq2[:], 0.0)
    b.op("vector", "memset", maxk2[:], 0.0, inc=("vV", 1))
    b.dma("sync", "dFlag", flag_sb[:], flag[:, :])

    def load_w(wdram):
        b.need("gpsimd", "wsb")
        nW = None
        for cc in range(4):
            nW = b.dma("gpsimd", "dW",
                       wsb[:].rearrange("p (c n) -> p c n", c=32)
                       [:, cc * 8:(cc + 1) * 8, :],
                       wdram.ap().rearrange("(c p) n -> p c n", p=128)
                       [:, cc * 8:(cc + 1) * 8, :])
        return nW

    def load_xnc(tc, want_cs):
        tp = tc % 2
        fd = feat_done[tc // 2]
        for i6 in range(5):
            b.wait("sync", f"dFeat{i6}", fd[i6])
        b.need("sync", f"xnc{tp}")
        nXC = None
        for cc in range(4):
            nXC = b.dma("sync", f"dXC{tp}",
                        xnc[tp][:].rearrange("p (c n) -> p c n", c=32)
                        [:, cc * 8:(cc + 1) * 8, :],
                        xn_feat[:, tc * 512:(tc + 1) * 512]
                        .rearrange("(c p) n -> p c n", p=128)
                        [:, cc * 8:(cc + 1) * 8, :])
        return nXC, None

    def qk_begin(wdram, kind):
        return {"normq": [], "prev_nvo": [0], "pend": {0: None, 1: None},
                "wready": load_w(wdram)}

    def qk_pass(wdram, kind, tc_list=None, st=None, flush=True):
        if tc_list is None:
            tc_list = list(range(DBG_NTC))
        if st is None:
            st = qk_begin(wdram, kind)
        normq = st["normq"]
        prev_nvo = st["prev_nvo"]

        def flush_norm(ent):
            sp_, h_, nsq_ = ent
            b.wait("tensor", "aA", nsq_)
            b.need("tensor", "R2")
            npn = b.op("tensor", "matmul", R2[:], ones128bf[:], sqn[sp_][:],
                       start=True, stop=True, inc=("pP", 1))
            b.released(f"sqn{sp_}", "pP", npn)
            b.wait("vector", "pP", npn)
            b.opw("vector", "reduce_max", mx[:], R2[:], AXX)
            tgt = maxq2 if kind == "q" else maxk2
            nvr = b.op("vector", "tensor_tensor", tgt[:, h_:h_ + 1],
                       tgt[:, h_:h_ + 1], mx[:], ALU.max, inc=("vV", 1))
            b.released("R2", "vV", nvr)

        wready = st["wready"]
        pend = st["pend"]
        for tc in tc_list:
            tp = tc % 2
            if pend.get(tp) is None or pend[tp][0] != tc:
                pend[tp] = (tc,) + load_xnc(tc, True)
            if tc + 1 < DBG_NTC and (tc + 1) // 2 in feat_done:
                pend[(tc + 1) % 2] = (tc + 1,) + load_xnc(tc + 1, True)
            _, nXC, _ = pend[tp]
            b.need("sync", "cs0")
            b.dma("sync", "dCS0", coss[0][:], cosT[:, tc * 512:(tc + 1) * 512])
            nCS = b.dma("sync", "dCS0", sins[0][:], sinT[:, tc * 512:(tc + 1) * 512])
            b.wait("tensor", f"dXC{tp}", nXC)
            if tc == 0:
                b.wait("tensor", "dW", wready)
            last_pmm = 0
            for h in range(DBG_NH):
                it = tc * NH + h
                pp = it % 3
                qp = it % 2
                sp = it % 3
                b.need("tensor", f"PRJ{pp}")
                for ic in range(32):
                    b.op("tensor", "matmul", PRJ[pp][:],
                         wsb[:, ic * 1024 + h * 128: ic * 1024 + h * 128 + 128],
                         xnc[tp][:, ic * 512:(ic + 1) * 512],
                         start=(ic == 0), stop=(ic == 31),
                         inc=("pP", 1) if ic == 31 else None)
                npmm = b.ctr["pP"]
                last_pmm = npmm
                if DBG_ROPE:
                    b.wait("vector", "pP", npmm)
                    b.wait("vector", "dCS0", nCS)
                    b.wait("vector", "gG", prev_nvo[0])
                    nv1 = b.op("vector", "tensor_tensor", rope1[:], PRJ[pp][:],
                               coss[tp][:], ALU.mult, inc=("vV", 1))
                    b.wait("scalar", "pP", npmm)
                    b.wait("scalar", "vV", nv1)
                    b.op("scalar", "activation", rt[0:64, :], PRJ[pp][64:128, :],
                         AF.Copy, scale=-1.0)
                    na1 = b.op("scalar", "activation", rt[64:128, :],
                               PRJ[pp][0:64, :], AF.Copy, inc=("aA", 1))
                    b.released(f"PRJ{pp}", "vV", nv1)
                    b.released(f"PRJ{pp}", "aA", na1)
                    b.wait("gpsimd", "aA", na1)
                    b.op("gpsimd", "tensor_tensor", rope2[:], rt[:], sins[tp][:],
                         ALU.mult, inc=("gG", 1))
                    b.wait("gpsimd", "gG", b.ctr["gG"])
                    b.wait("gpsimd", "vV", nv1)
                    b.need("gpsimd", f"qout{qp}")
                    nvo = b.op("gpsimd", "tensor_tensor", qout[qp][:], rope1[:],
                               rope2[:], ALU.add, inc=("gG", 1))
                    prev_nvo[0] = nvo
                else:
                    b.wait("gpsimd", "pP", npmm)
                    b.need("gpsimd", f"qout{qp}")
                    nvo = b.op("gpsimd", "tensor_copy", qout[qp][:], PRJ[pp][:],
                               inc=("gG", 1))
                    b.released(f"PRJ{pp}", "gG", nvo)
                if h == NH - 1:
                    b.released("cs0", "gG", nvo)
                if kind == "q" and DBG_QBAR:
                    lo = max(TEXT_LO, tc * 512) - tc * 512
                    hi = min(TEXT_HI, (tc + 1) * 512) - tc * 512
                    if hi > lo:
                        b.wait("vector", "gG", nvo)
                        b.opw("vector", "reduce_sum", qtmp[:],
                              qout[qp][:, lo:hi], AXX)
                        b.op("vector", "tensor_tensor", qbar[:, h:h + 1],
                             qbar[:, h:h + 1], qtmp[:], ALU.add)
                if DBG_NORMS:
                    b.need("scalar", f"sqn{sp}")
                    b.wait("scalar", "gG", nvo)
                    nsq = b.op("scalar", "activation", sqn[sp][:], qout[qp][:],
                               AF.Square, inc=("aA", 1))
                    b.released(f"qout{qp}", "aA", nsq)
                    normq.append((sp, h, nsq))
                    if len(normq) > 2:
                        flush_norm(normq.pop(0))
                b.wait("sync", "gG", nvo)
                dst = qbuf if kind == "q" else kbuf
                nst = b.dma("sync", f"dSp{qp}",
                            dst[h * 128:(h + 1) * 128, tc * 512:(tc + 1) * 512],
                            qout[qp][:])
                b.released(f"qout{qp}", f"dSp{qp}", nst)
            b.released(f"xnc{tp}", "pP", last_pmm)
        if flush:
            while normq:
                flush_norm(normq.pop(0))
            b.released("wsb", "pP", b.ctr["pP"])

    def v_pass(wdram):
        wready = load_w(wdram)
        pend = {0: None, 1: None}
        pend[0] = load_xnc(0, False)
        for tc in range(8):
            tp = tc % 2
            if tc + 1 < 8:
                pend[(tc + 1) % 2] = load_xnc(tc + 1, False)
            nXC, _ = pend[tp]
            b.wait("tensor", f"dXC{tp}", nXC)
            if tc == 0:
                b.wait("tensor", "dW", wready)
            for tt in range(4):
                for half in range(2):
                    it = tc * 8 + tt * 2 + half
                    pp = it % 3
                    qp = it % 2
                    b.need("tensor", f"PRJ{pp}")
                    for ic in range(32):
                        b.op("tensor", "matmul", PRJ[pp][:],
                             xnc[tp][:, ic * 512 + tt * 128: ic * 512 + tt * 128 + 128],
                             wsb[:, ic * 1024 + half * 512: ic * 1024 + half * 512 + 512],
                             start=(ic == 0), stop=(ic == 31),
                             inc=("pP", 1) if ic == 31 else None)
                    npmm = b.ctr["pP"]
                    b.wait("scalar", "pP", npmm)
                    b.need("scalar", f"qout{qp}")
                    nev = b.op("scalar", "activation", qout[qp][:], PRJ[pp][:],
                               AF.Copy, inc=("aA", 1))
                    b.released(f"PRJ{pp}", "aA", nev)
                    kc = tc * 4 + tt
                    b.wait("sync", "aA", nev)
                    nst = b.dma("sync", f"dSp{qp}",
                                vtok.ap()[half * 4:(half + 1) * 4, :, kc:kc + 1, :]
                                .rearrange("h p q d -> p h q d"),
                                qout[qp][:].rearrange("p (h q d) -> p h q d",
                                                      h=4, q=1))
                    b.released(f"qout{qp}", f"dSp{qp}", nst)
            b.released(f"xnc{tp}", "pP", b.ctr["pP"])
        b.released("wsb", "pP", b.ctr["pP"])

    feat_done = {}
    b.op("vector", "memset", epscol[:], 1e-5, inc=("vV", 1))
    qst = qk_begin(qwT, "q")
    nxs = [None] * 33
    nxs[0] = b.dma("sync", "dX0", x_sb[0][:], x_b[0:128, :])
    for i in range(32):
        par = i % 2
        if i + 1 < 32:
            np_ = (i + 1) % 2
            b.need("sync", f"x{np_}")
            nxs[i + 1] = b.dma("sync", f"dX{np_}", x_sb[np_][:],
                               x_b[(i + 1) * 128:(i + 2) * 128, :])
        b.wait("vector", f"dX{par}", nxs[i])
        for cc in range(8):
            nbs_ = b.op("vector", "bn_stats", stt[par][:, cc * 6:(cc + 1) * 6],
                        x_sb[par][:, cc * 512:(cc + 1) * 512],
                        inc=("vV", 1) if cc == 7 else None)
        b.wait("vector", "vV", nbs_)
        b.opw("vector", "bn_aggr", agb[par][:], stt[par][:])
        nv = b.op("vector", "tensor_scalar", negmean[par][:],
                  agb[par][:, 0:1], -1.0, None, ALU.mult, inc=("vV", 1))
        b.wait("scalar", "vV", nv)
        b.opw("scalar", "activation", lnv[par][:], agb[par][:, 1:2], AF.Ln,
              bias=epscol[:])
        na = b.op("scalar", "activation", rstd[par][:], lnv[par][:], AF.Exp,
                  scale=-0.5, inc=("aA", 1))
        b.wait("vector", "aA", na)
        b.need("vector", f"xn{par}")
        nv = b.op("vector", "tensor_scalar", xn_sb[par][:], x_sb[par][:],
                  negmean[par][:], rstd[par][:], ALU.add, ALU.mult,
                  inc=("vV", 1))
        b.released(f"x{par}", "vV", nv)
        b.wait("sync", "vV", nv)
        nst = b.dma("sync", f"dXnT{par}", xn_tok[i * 128:(i + 1) * 128, :],
                    xn_sb[par][:])
        b.released(f"xn{par}", f"dXnT{par}", nst)
        if i % 8 == 7:
            g = i // 8
            b.wait("scalar", "dXnT0", b.ctr["dXnT0"])
            b.wait("scalar", "dXnT1", b.ctr["dXnT1"])
            ntrs = [None] * 32
            for f in range(32):
                tp = f % 5
                b.need("scalar", f"xtr{tp}")
                ntrs[f] = b.dma("scalar", f"dXtr{tp}", xtr[tp][:],
                                xn_tok[g * 1024:(g + 1) * 1024,
                                       f * 128:(f + 1) * 128], transpose=True)
                if f >= 3:
                    fw = f - 3
                    wp = fw % 5
                    b.wait("scalar", f"dXtr{wp}", ntrs[fw])
                    nwr = b.dma("scalar", f"dFeat{wp}",
                                xn_feat[fw * 128:(fw + 1) * 128,
                                        g * 1024:(g + 1) * 1024], xtr[wp][:])
                    b.released(f"xtr{wp}", f"dFeat{wp}", nwr)
            for fw in range(29, 32):
                wp = fw % 5
                b.wait("scalar", f"dXtr{wp}", ntrs[fw])
                nwr = b.dma("scalar", f"dFeat{wp}",
                            xn_feat[fw * 128:(fw + 1) * 128,
                                    g * 1024:(g + 1) * 1024], xtr[wp][:])
                b.released(f"xtr{wp}", f"dFeat{wp}", nwr)
            feat_done[g] = tuple(b.ctr[f"dFeat{i}"] for i in range(5))
            if upto != 0:
                qk_pass(qwT, "q", [2 * g, 2 * g + 1], qst, flush=(g == 3))
    if upto == 0:
        for i6 in range(5):
            b.wait("sync", f"dFeat{i6}", b.ctr[f"dFeat{i6}"])
        nc.sbuf_base, nc.psum_base = sbP, psP
        b.run_block()
        _cut(0)
        return nc

    if sub == "q":
        return _cut1()
    qk_pass(kwT, "k")
    if sub == "qk":
        return _cut1()

    nvq = b.op("vector", "tensor_copy", qbar_bf[:], qbar[:], inc=("vV", 1))
    b.wait("sync", "dSp0", b.ctr["dSp0"])
    b.wait("sync", "dSp1", b.ctr["dSp1"])
    b.wait("tensor", "vV", nvq)
    for h in range(NH):
        b.need("sync", "kvis")
        nkv = b.dma("sync", "dKeep", kvis[:], kbuf[h * 128:(h + 1) * 128, 1:513])
        b.wait("tensor", "dKeep", nkv)
        npi = b.op("tensor", "matmul", R1[:], qbar_bf[:, h:h + 1], kvis[:],
                   start=(h == 0), stop=(h == NH - 1), inc=("pP", 1))
        b.released("kvis", "pP", npi)
    b.wait("vector", "pP", b.ctr["pP"])
    b.wait("vector", "dFlag", 16)
    nvi = b.op("vector", "tensor_scalar", imp_sb[:], R1[:], flag_sb[0:1, 0:1],
               None, ALU.mult, inc=("vV", 1))
    b.wait("sync", "vV", nvi)
    b.dma("sync", "dImp", imp_in[:, :], imp_sb[:])
    b.wait("gpsimd", "dImp", b.ctr["dImp"])
    b.op("gpsimd", "collective_compute", "AllReduce", ALU.add,
         replica_groups=[[0, 1, 2, 3, 4, 5, 6, 7]],
         ins=[imp_in.ap().opt()], outs=[imp_out.ap().opt()], inc=("cC", 1))

    if sub == "qki":
        return _cut1()
    v_pass(vwT)
    if sub == "qkiv":
        return _cut1()

    b.wait("sync", "cC", 1)
    b.wait("sync", "vV", 1)
    b.dma("sync", "dKeep", imp2_sb[:], imp_out[:, :])
    b.dma("sync", "dKeep", impshift[1:128, 0:1],
          imp_out[0, 0:127].rearrange("(r p) -> p r", p=127),
          allow_slow_non_contiguous=True)
    b.dma("sync", "dKeep", impshift[:, 1:4],
          imp_out[0, 127:511].rearrange("(r p) -> p r", p=128),
          allow_slow_non_contiguous=True)
    nks = b.dma("sync", "dKeep", impshift[0:1, 4:5],
                imp_out[0, 511:512].rearrange("(r p) -> p r", p=1),
                allow_slow_non_contiguous=True)
    b.wait("tensor", "dKeep", nks)
    b.need("tensor", "PRJ0")
    nbm = b.op("tensor", "matmul", PRJ[0][:], onesrowA[:], imp2_sb[:],
               start=True, stop=True, inc=("pP", 1))
    b.wait("scalar", "pP", nbm)
    nbs = b.op("scalar", "activation", Bsb[:], PRJ[0][:], AF.Copy, inc=("aA", 1))
    b.wait("vector", "aA", nbs)
    b.wait("vector", "dKeep", nks)
    b.wait("vector", "vV", nvi)
    for kc in range(4):
        b.opw("vector", "tensor_scalar", cmpscr[:], Bsb[:],
              impshift[:, kc:kc + 1], None, ALU.is_gt)
        b.opw("vector", "reduce_sum", rankc[:], cmpscr[:], AXX)
        b.op("vector", "tensor_scalar", keepf32[:, kc:kc + 1], rankc[:],
             255.5, None, ALU.is_lt)
    b.op("vector", "memset", keepf32[:, 4:6], 1.0)
    b.opw("vector", "tensor_scalar", cmpscr[0:1, :], Bsb[0:1, :],
          impshift[0:1, 4:5], None, ALU.is_gt)
    b.opw("vector", "reduce_sum", r512[:], cmpscr[0:1, :], AXX)
    b.op("vector", "tensor_scalar", keepf32[0:1, 4:5], r512[:],
         255.5, None, ALU.is_lt)
    b.opw("vector", "memset", keepf32[0:1, 0:1], 1.0)
    b.op("vector", "tensor_copy", keepfull[:], keepf32[:], inc=("vV", 1))
    b.wait("sync", "dSp0", b.ctr["dSp0"])
    b.wait("sync", "dSp1", b.ctr["dSp1"])
    for i6 in range(5):
        b.wait("sync", f"dFeat{i6}", b.ctr[f"dFeat{i6}"])
    nc.sbuf_base, nc.psum_base = sbP, psP
    b.run_block()
    if _cut(1):
        return nc

    b.start_block_waits()
    masks_sb = A("masks_sb", [128, 4 * 512], BF16)
    qh = [A("qh0", [128, S], BF16), A("qh1", [128, S], BF16)]
    Kh = [A("Kh0", [128, S], BF16), A("Kh1", [128, S], BF16)]
    Vh = [A("Vh0", [128, 32 * D], BF16), A("Vh1", [128, 32 * D], BF16)]
    pt = [A(f"pt{i}", [128, 512], BF16) for i in range(4)]
    rdb = [A("rdb0", [128, 512], F32), A("rdb1", [128, 512], F32)]
    attout = [A("attout0", [128, 512], BF16), A("attout1", [128, 512], BF16)]
    nm1 = A("nm1", [1, 1], F32)
    lnm = A("lnm", [1, 1], F32)
    ch = A("ch", [1, 1], F32)
    negc1 = A("negc1", [1, 1], F32)
    negcb = [A("negcb0", [128, 1], F32), A("negcb1", [128, 1], F32)]
    rcp = [A("rcp0", [1, 512], F32), A("rcp1", [1, 512], F32)]
    onesrow2 = A("onesrow2", [1, 128], F32)
    wnat = [A("wnat0", [128, HID], F32), A("wnat1", [128, HID], F32)]
    xfull = A("xfull", [128, HID], F32)
    sfull = A("sfull", [128, HID], F32)
    wqn = [A("wqn0", [128, HID], BF16), A("wqn1", [128, HID], BF16)]
    absm = A("absm", [128, 32], F32)
    sclq = A("sclq", [128, 32], F32)
    rinv = A("rinv", [128, 32], F32)
    gcol = A("gcol", [128, 1], F32)
    Spsum = [P(f"S{i}_", [128, 512], F32) for i in range(4)]
    CPV = [P("CPV0", [128, 512], F32), P("CPV1", [128, 512], F32)]
    DBC = P("DBC", [128, 512], F32)
    R1a = P("R1a", [1, 512], F32)
    NEG = DBC

    b.op("vector", "memset", onesrow2[:], 1.0, inc=("vV", 1))
    nmask = b.dma("sync", "dKeep", masks_sb[:].rearrange("p (o c) -> p o c", o=4),
                  masks.ap().rearrange("o p c -> p o c"))
    b.wait("vector", "dKeep", nmask)
    b.op("gpsimd", "memset", gcol[:], 1.0, inc=("gG", 1))
    b.wait("gpsimd", "gG", b.ctr["gG"])
    b.op("gpsimd", "tensor_scalar", gcol[:], gcol[:], C52, -C52, ALU.add,
         ALU.add, inc=("gG", 1))
    b.wait("gpsimd", "gG", b.ctr["gG"])
    nwl = [None] * 8

    def gw(meth, *a, **k):
        v = b.op("gpsimd", meth, *a, inc=("gG", 1), **k)
        b.wait("gpsimd", "gG", v)
        return v

    def quant_tile(ot):
        par = ot % 2
        b.wait("vector", f"dWn{par}", nwl[ot])
        b.opw("vector", "tensor_reduce", absm[:],
              wnat[par][:].rearrange("p (g c) -> p g c", c=128), AXX, ALU.max,
              apply_absolute_value=True)
        b.opw("vector", "tensor_scalar", sclq[:], absm[:], 1.0 / 7.0, None,
              ALU.mult)
        b.opw("vector", "tensor_scalar", absm[:], absm[:], 1e-30, None, ALU.max)
        b.opw("vector", "reciprocal", rinv[:], absm[:])
        nv_ = b.op("vector", "tensor_scalar", rinv[:], rinv[:], 7.0, None,
                   ALU.mult, inc=("vV", 1))
        b.wait("gpsimd", "vV", nv_)
        for g in range(32):
            ngx = b.op("gpsimd", "tensor_scalar",
                       xfull[:, g * 128:(g + 1) * 128],
                       wnat[par][:, g * 128:(g + 1) * 128], rinv[:, g:g + 1],
                       None, ALU.mult, inc=("gG", 1) if g == 31 else None)
        b.wait("gpsimd", "gG", ngx)
        nrel = gw("tensor_scalar", xfull[:], xfull[:], C23, -C23, ALU.add,
                  ALU.add)
        b.released(f"wnat{par}", "gG", nrel)
        gw("tensor_scalar", sfull[:], xfull[:], C52, -C52, ALU.add, ALU.add)
        gw("tensor_tensor", sfull[:], sfull[:], xfull[:], ALU.subtract)
        gw("tensor_scalar", sfull[:], sfull[:], gcol[:], None, ALU.mult)
        gw("tensor_tensor", xfull[:], sfull[:], xfull[:], ALU.add)
        b.need("gpsimd", f"wqn{par}")
        nq2 = None
        for g in range(32):
            nq2 = b.op("gpsimd", "tensor_scalar",
                       wqn[par][:, g * 128:(g + 1) * 128],
                       xfull[:, g * 128:(g + 1) * 128], sclq[:, g:g + 1], None,
                       ALU.mult, inc=("gG", 1) if g == 31 else None)
        b.wait("sync", "gG", nq2)
        nst = b.dma("sync", f"dWqn{par}", wq_nat[ot * 128:(ot + 1) * 128, :],
                    wqn[par][:])
        b.released(f"wqn{par}", f"dWqn{par}", nst)

    def att_loads(h):
        hp = h % 2
        b.need("sync", f"qh{hp}")
        nq = b.dma("sync", f"dQH{hp}", qh[hp][:], qbuf[h * 128:(h + 1) * 128, :])
        b.need("sync", f"Kh{hp}")
        nk = b.dma("sync", f"dKH{hp}", Kh[hp][:], kbuf[h * 128:(h + 1) * 128, :])
        b.need("sync", f"Vh{hp}")
        nv_ = b.dma("sync", f"dVH{hp}",
                    Vh[hp][:].rearrange("p (o kc d) -> p o kc d", o=1, kc=32),
                    vtok.ap()[h:h + 1, :, :, :].rearrange("o p kc d -> p o kc d"))
        return nq, nk, nv_

    pend = {0: att_loads(0), 1: None}
    pending_tail = []

    def emit_tail():
        while pending_tail:
            h_, j_, nrc_ = pending_tail.pop(0)
            jp_ = j_ % 2
            b.need("tensor", "DBC")
            b.wait("tensor", "vV", nrc_)
            nbc = b.op("tensor", "matmul", DBC[:], onesrow2[:], rcp[jp_][:],
                       start=True, stop=True, inc=("pP", 1))
            b.released(f"rcp{jp_}", "pP", nbc)
            b.wait("scalar", "pP", nbc)
            b.need("scalar", f"rdb{jp_}")
            nrd = b.op("scalar", "activation", rdb[jp_][:], DBC[:], AF.Copy,
                       inc=("aA", 1))
            b.released("DBC", "aA", nrd)
            b.wait("vector", "aA", nrd)
            b.need("vector", f"attout{jp_}")
            nao = b.op("vector", "tensor_tensor", attout[jp_][:], CPV[jp_][:],
                       rdb[jp_][:], ALU.mult, inc=("vV", 1))
            b.released(f"rdb{jp_}", "vV", nao)
            b.released(f"CPV{jp_}", "vV", nao)
            b.wait("sync", "vV", nao)
            nst = b.dma("sync", f"dAtt{jp_}",
                        att_mine[h_ * 128:(h_ + 1) * 128,
                                 j_ * 512:(j_ + 1) * 512],
                        attout[jp_][:])
            b.released(f"attout{jp_}", f"dAtt{jp_}", nst)

    for h in range(NH):
        hp = h % 2
        b.need("sync", f"wnat{h % 2}")
        nwl[h] = b.dma("sync", f"dWn{h % 2}", wnat[h % 2][:],
                       fw_nat[h * 128:(h + 1) * 128, :])
        if h + 1 < NH:
            pend[(h + 1) % 2] = att_loads(h + 1)
        nq, nk, nv_ = pend[hp]
        b.opw("vector", "tensor_tensor", nm1[:], maxq2[:, h:h + 1],
              maxk2[:, h:h + 1], ALU.mult)
        nvn = b.op("vector", "tensor_scalar", nm1[:], nm1[:], 1e-30, None,
                   ALU.max, inc=("vV", 1))
        b.wait("scalar", "vV", nvn)
        b.opw("scalar", "activation", lnm[:], nm1[:], AF.Ln)
        b.opw("scalar", "activation", ch[:], lnm[:], AF.Exp, scale=0.5)
        nac = b.op("scalar", "mul", negc1[:], ch[:], -SCALE, inc=("aA", 1))
        b.wait("tensor", "aA", nac)
        b.need("tensor", "DBC")
        nnp = b.op("tensor", "matmul", NEG[:, 0:1], onesrow2[:], negc1[:],
                   start=True, stop=True, inc=("pP", 1))
        b.wait("scalar", "pP", nnp)
        nnc = b.op("scalar", "activation", negcb[hp][:], NEG[:, 0:1], AF.Copy,
                   inc=("aA", 1))
        b.wait("scalar", "aA", nnc)
        b.released("DBC", "aA", nnc)
        b.wait("vector", f"dVH{hp}", nv_)
        nvz = None
        for kc in range(5):
            nvz = b.op("vector", "tensor_scalar",
                       Vh[hp][:, kc * D:(kc + 1) * D],
                       Vh[hp][:, kc * D:(kc + 1) * D],
                       keepf32[:, kc:kc + 1], None, ALU.mult, inc=("vV", 1))
        b.wait("tensor", "vV", nvz)
        b.wait("tensor", f"dQH{hp}", nq)
        b.wait("tensor", f"dKH{hp}", nk)
        for j in range(8):
            kcn = 4 * j + 4
            jp = j % 2

            def emit_S(kc, hp=hp, j=j):
                sp_ = kc % 4
                b.need("tensor", f"S{sp_}")
                return b.op("tensor", "matmul", Spsum[sp_][:],
                            Kh[hp][:, kc * 128:(kc + 1) * 128],
                            qh[hp][:, j * 512:(j + 1) * 512],
                            start=True, stop=True, inc=("pP", 1))
            ns_of = {}
            for kc in range(min(4, kcn)):
                ns_of[kc] = emit_S(kc)
            emit_tail()
            b.need("tensor", "R1a")
            b.need("tensor", f"CPV{jp}")
            for kc in range(kcn):
                sp_ = kc % 4
                ns = ns_of.pop(kc)
                b.wait("scalar", "pP", ns)
                if j == 0 and kc == 0:
                    b.wait("scalar", "aA", nnc)
                b.need("scalar", f"pt{sp_}")
                nex = b.op("scalar", "activation", pt[sp_][:], Spsum[sp_][:],
                           AF.Exp, bias=negcb[hp][:], scale=SCALE,
                           inc=("aA", 1))
                b.released(f"S{sp_}", "aA", nex)
                gate = ("aA", nex)
                if kc >= 4 * j:
                    o = kc - 4 * j
                    b.wait("vector", "aA", nex)
                    nmk = b.op("vector", "tensor_tensor", pt[sp_][:], pt[sp_][:],
                               masks_sb[:, o * 512:(o + 1) * 512], ALU.mult,
                               inc=("vV", 1))
                    gate = ("vV", nmk)
                b.wait("tensor", gate[0], gate[1])
                kk = min(kc, 5)
                b.op("tensor", "matmul", R1a[:], keepfull[:, kk:kk + 1],
                     pt[sp_][:], start=(kc == 0), stop=(kc == kcn - 1))
                npv = b.op("tensor", "matmul", CPV[jp][:],
                           Vh[hp][:, kc * D:(kc + 1) * D], pt[sp_][:],
                           start=(kc == 0), stop=(kc == kcn - 1), inc=("pP", 1))
                b.released(f"pt{sp_}", "pP", npv)
                if kc + 4 < kcn:
                    ns_of[kc + 4] = emit_S(kc + 4)
            b.wait("vector", "pP", b.ctr["pP"])
            b.need("vector", f"rcp{jp}")
            nrc = b.op("vector", "reciprocal", rcp[jp][:], R1a[:], inc=("vV", 1))
            b.released("R1a", "vV", nrc)
            pending_tail.append((h, j, nrc))
        b.released(f"qh{hp}", "pP", b.ctr["pP"])
        b.released(f"Kh{hp}", "pP", b.ctr["pP"])
        b.released(f"Vh{hp}", "pP", b.ctr["pP"])
        quant_tile(h)
    emit_tail()
    b.wait("gpsimd", "dAtt0", b.ctr["dAtt0"])
    b.wait("gpsimd", "dAtt1", b.ctr["dAtt1"])
    b.op("gpsimd", "collective_compute", "AllGather", ALU.bypass,
         replica_groups=[[0, 1, 2, 3], [4, 5, 6, 7]],
         ins=[att_mine.ap().opt()], outs=[att_full.ap().opt()], inc=("cC", 1))
    nc.sbuf_base, nc.psum_base = sbP, psP
    b.run_block()
    if _cut(2):
        return nc

    b.start_block_waits()
    wqT_sb = A("wqT_sb", [128, 32 * 1024], BF16)
    sbP3, psP3 = nc.sbuf_base, nc.psum_base
    b.wait("sync", "dWqn0", b.ctr["dWqn0"])
    b.wait("sync", "dWqn1", b.ctr["dWqn1"])
    for ic in range(32):
        for oc in range(2):
            b.dma("sync", "dWqT",
                  wqT_sb[:, ic * 1024 + oc * 512: ic * 1024 + oc * 512 + 512],
                  wq_nat[oc * 512:(oc + 1) * 512, ic * 128:(ic + 1) * 128],
                  transpose=True)
    nc.sbuf_base, nc.psum_base = sbP3, psP3
    b.run_block()
    if _cut(3):
        return nc

    b.start_block_waits()
    osb = A("osb", [128, 32 * 1024], BF16)
    att_c = [A("att_c0", [128, 32 * 256], BF16), A("att_c1", [128, 32 * 256], BF16)]
    xnc3 = [A("xnc30", [128, 32 * 256], BF16), A("xnc31", [128, 32 * 256], BF16)]
    xresb = A("xresb", [128, 1024], F32)
    outb = [A("outb0", [128, 512], F32), A("outb1", [128, 512], F32)]
    biasv = A("biasv", [1, FS], F32)
    biasB = A("biasB", [128, FS], BF16)
    onesrow3 = A("onesrow3", [1, 128], F32)
    P3 = [P("P3A", [128, 512], F32), P("P3B", [128, 512], F32)]

    nv3 = b.op("vector", "memset", onesrow3[:], 1.0, inc=("vV", 1))
    b.dma("sync", "dBias", biasv[:], fbias[:, :])
    for cc in range(4):
        b.dma("gpsimd", "dOsb",
              osb[:].rearrange("p (c n) -> p c n", c=32)[:, cc * 8:(cc + 1) * 8, :],
              owT.ap().rearrange("(c p) n -> p c n", p=128)[:, cc * 8:(cc + 1) * 8, :])
    b.wait("sync", "cC", 2)
    b.wait("tensor", "dBias", b.ctr["dBias"])
    b.wait("tensor", "vV", nv3)
    for fo in range(2):
        nb1 = b.op("tensor", "matmul", P3[fo][:], onesrow3[:],
                   biasv[0:1, fo * 512:(fo + 1) * 512], start=True, stop=True,
                   inc=("pP", 1))
        b.wait("scalar", "pP", nb1)
        nb2 = b.op("scalar", "activation", biasB[:, fo * 512:(fo + 1) * 512],
                   P3[fo][:], AF.Copy, inc=("aA", 1))
        b.released(f"P3{fo}", "aA", nb2)
    b.wait("tensor", "dWqT", b.ctr["dWqT"])
    b.wait("tensor", "dOsb", b.ctr["dOsb"])

    def p3_loads(tc):
        tp = tc % 2
        t0 = tc * 256
        b.need("sync", f"att_c{tp}")
        nac_ = None
        for cc in range(2):
            nac_ = b.dma("sync", f"dAC{tp}",
                         att_c[tp][:].rearrange("p (c n) -> p c n", c=32)
                         [:, cc * 16:(cc + 1) * 16, :],
                         att_full[:, t0:t0 + 256]
                         .rearrange("(c p) n -> p c n", p=128)
                         [:, cc * 16:(cc + 1) * 16, :])
        b.need("sync", f"xnc3{tp}")
        nxc3 = None
        for cc in range(2):
            nxc3 = b.dma("sync", f"dXn3{tp}",
                         xnc3[tp][:].rearrange("p (c n) -> p c n", c=32)
                         [:, cc * 16:(cc + 1) * 16, :],
                         xn_feat[:, t0:t0 + 256]
                         .rearrange("(c p) n -> p c n", p=128)
                         [:, cc * 16:(cc + 1) * 16, :])
        return nac_, nxc3

    pend3 = {0: p3_loads(0), 1: None}
    for tc in range(16):
        tp = tc % 2
        t0 = tc * 256
        if tc + 1 < 16:
            pend3[(tc + 1) % 2] = p3_loads(tc + 1)
        nac_, nxc3 = pend3[tp]
        b.wait("tensor", f"dAC{tp}", nac_)
        b.wait("tensor", f"dXn3{tp}", nxc3)
        for tt in range(2):
            trow = t0 + tt * 128
            b.need("sync", "xres")
            nxr = b.dma("sync", "dXr0", xresb[:],
                        xres_i[trow:trow + 128, :])
            b.wait("vector", "dXr0", nxr)
            for fo in range(2):
                pp = (tc * 4 + tt * 2 + fo) % 2
                op_ = pp
                b.need("tensor", f"P3{pp}")
                for hc in range(32):
                    b.op("tensor", "matmul", P3[pp][:],
                         att_c[tp][:, hc * 256 + tt * 128: hc * 256 + tt * 128 + 128],
                         osb[:, hc * 1024 + fo * 512: hc * 1024 + fo * 512 + 512],
                         start=(hc == 0), stop=False)
                for ic in range(32):
                    b.op("tensor", "matmul", P3[pp][:],
                         xnc3[tp][:, ic * 256 + tt * 128: ic * 256 + tt * 128 + 128],
                         wqT_sb[:, ic * 1024 + fo * 512: ic * 1024 + fo * 512 + 512],
                         start=False, stop=(ic == 31),
                         inc=("pP", 1) if ic == 31 else None)
                npf = b.ctr["pP"]
                b.wait("vector", "pP", npf)
                b.need("vector", f"outb{op_}")
                b.opw("vector", "tensor_tensor", outb[op_][:], P3[pp][:],
                      biasB[:, fo * 512:(fo + 1) * 512], ALU.add)
                nvo = b.op("vector", "tensor_tensor", outb[op_][:], outb[op_][:],
                           xresb[:, fo * 512:(fo + 1) * 512], ALU.add,
                           inc=("vV", 1))
                b.released(f"P3{pp}", "vV", nvo)
                if fo == 1:
                    b.released("xres", "vV", nvo)
                b.wait("sync", "vV", nvo)
                nyw = b.dma("sync", f"dY{op_}",
                            y[trow:trow + 128, fo * 512:(fo + 1) * 512],
                            outb[op_][:])
                b.released(f"outb{op_}", f"dY{op_}", nyw)
        b.released(f"att_c{tp}", "pP", b.ctr["pP"])
        b.released(f"xnc3{tp}", "pP", b.ctr["pP"])
    b.wait("sync", "dY0", b.ctr["dY0"])
    b.wait("sync", "dY1", b.ctr["dY1"])
    b.run_block()

    nc.finalize()
    return nc


_NC_CACHE = []


def _host_tables():
    inv = (10000.0 ** (-(np.arange(64, dtype=np.float64) / 64.0)))
    t = np.arange(S, dtype=np.float64)
    ang = t[None, :] * inv[:, None]
    ang = np.concatenate([ang, ang], axis=0)
    cosTv = np.cos(ang).astype(np.float32)
    sinTv = np.sin(ang).astype(np.float32)
    p = np.arange(128)[:, None]
    c = np.arange(512)[None, :]
    m = np.stack([(p + 128 * o <= c) for o in range(4)], 0)
    masksv = m.astype(ml_dtypes.bfloat16)
    return cosTv, sinTv, masksv


def kernel(L0_attn_hidden_states, L0_attn_q_w, L0_attn_k_w, L0_attn_v_w,
           L0_attn_o_w, L0_ffn_weight, L0_ffn_bias):
    x = np.asarray(L0_attn_hidden_states)
    qwTv = np.ascontiguousarray(np.asarray(L0_attn_q_w).T)
    kwTv = np.ascontiguousarray(np.asarray(L0_attn_k_w).T)
    vwTv = np.ascontiguousarray(np.asarray(L0_attn_v_w).T)
    owTv = np.ascontiguousarray(np.asarray(L0_attn_o_w).T)
    fw = np.asarray(L0_ffn_weight)
    fb = np.asarray(L0_ffn_bias)
    cosTv, sinTv, masksv = _host_tables()

    if not _NC_CACHE:
        _NC_CACHE.append(build())
    nc = _NC_CACHE[0]

    in_maps = []
    for c in range(8):
        bidx, r = c // 4, c % 4
        fs = slice(r * FS, (r + 1) * FS)
        in_maps.append({
            "x_b": np.ascontiguousarray(x[bidx]),
            "xres_i": np.ascontiguousarray(x[bidx][:, fs]),
            "qwT": np.ascontiguousarray(qwTv[:, fs]),
            "kwT": np.ascontiguousarray(kwTv[:, fs]),
            "vwT": np.ascontiguousarray(vwTv[:, fs]),
            "owT": np.ascontiguousarray(owTv[:, fs]),
            "fw_nat": np.ascontiguousarray(fw[fs, :]),
            "fbias": np.ascontiguousarray(fb[fs][None, :]),
            "cosT": cosTv, "sinT": sinTv, "masks": masksv,
            "flag": np.array([[1.0 if bidx == 0 else 0.0]], np.float32),
        })
    res = run_bass_kernel_spmd(nc, in_maps, core_ids=list(range(8)))
    out = np.empty((2, S, HID), np.float32)
    for c in range(8):
        bidx, r = c // 4, c % 4
        out[bidx][:, r * FS:(r + 1) * FS] = np.asarray(res.results[c]["y"])
    return out


# version 11
# speedup vs baseline: 1.3411x; 1.3411x over previous
import numpy as np
import ml_dtypes
import concourse.bass as bass
import concourse.mybir as mybir
from concourse.bass_utils import run_bass_kernel_spmd

F32 = mybir.dt.float32
BF16 = mybir.dt.bfloat16
AF = mybir.ActivationFunctionType
ALU = mybir.AluOpType
AXX = mybir.AxisListType.X

S = 4096
HID = 4096
D = 128
NH = 8
FS = 1024
SCALE = 0.08838834764831845
C23 = float(np.float32(1.5 * 2.0 ** 23))
C52 = float(np.float32(1.5 * 2.0 ** 52))
TEXT_LO, TEXT_HI = 513, 4040
DBG_NTC = 8
DBG_NH = 8
DBG_NORMS = True
DBG_QBAR = True
DBG_ROPE = True


class B:
    ENGS = ("sync", "gpsimd", "vector", "scalar", "tensor")

    def __init__(self, nc):
        self.nc = nc
        self.prog = {e: [] for e in self.ENGS}
        self.sems = {}
        self.ctr = {}
        self.rel = {}
        self.nbar = 0

    def sem(self, name):
        if name not in self.sems:
            self.sems[name] = self.nc.alloc_semaphore(name=name)
            self.ctr[name] = 0
        return self.sems[name]

    def op(self, eng, meth, *args, inc=None, **kw):
        s = self.sems[inc[0]] if inc else None
        if inc:
            self.ctr[inc[0]] += inc[1]

        def fn(e, meth=meth, args=args, kw=kw, s=s, iv=inc[1] if inc else None):
            ins = getattr(e, meth)(*args, **kw)
            if s is not None:
                ins.then_inc(s, iv)
            return ins
        self.prog[eng].append(fn)
        return self.ctr[inc[0]] if inc else None

    def opw(self, eng, meth, *args, **kw):
        sem = {"vector": "vV", "scalar": "aA", "tensor": "pP",
               "gpsimd": "gG"}[eng]
        v = self.op(eng, meth, *args, inc=(sem, 1), **kw)
        self.wait(eng, sem, v)
        return v

    def wait(self, eng, semname, val):
        if val is None or val <= 0:
            return
        s = self.sems[semname]
        self.prog[eng].append(lambda e, s=s, v=val: e.wait_ge(s, v))

    def dma(self, eng, semname, out, in_, **kw):
        self.sem(semname)
        return self.op(eng, "dma_start", out, in_, inc=(semname, 16), **kw)

    def released(self, key, semname, val):
        self.rel.setdefault(key, []).append((semname, val))

    def need(self, eng, key):
        for (s, v) in self.rel.pop(key, []):
            self.wait(eng, s, v)

    def run_block(self):
        self.sem("bar")
        self.nbar += 1
        for e in self.ENGS:
            if e != "sync":
                self.prog[e].append(lambda E: E.drain())
            s = self.sems["bar"]
            self.prog[e].append(lambda E, s=s: E.sem_inc(s, 1))
        nc = self.nc
        with nc.Block() as block:
            for ename, reg in (("sync", block.sync), ("gpsimd", block.gpsimd),
                               ("vector", block.vector), ("scalar", block.scalar),
                               ("tensor", block.tensor)):
                lst = self.prog[ename]

                def body(E, lst=lst):
                    for fn in lst:
                        fn(E)
                reg(body)
        self.prog = {e: [] for e in self.ENGS}

    def start_block_waits(self):
        for e in self.ENGS:
            self.wait(e, "bar", 5 * self.nbar)


def build(upto=4, sub=None):
    nc = bass.Bass("TRN2", target_bir_lowering=False)

    x_b = nc.dram_tensor("x_b", [S, HID], F32, kind="ExternalInput")
    xres_i = nc.dram_tensor("xres_i", [S, FS], F32, kind="ExternalInput")
    qwT = nc.dram_tensor("qwT", [HID, FS], F32, kind="ExternalInput")
    kwT = nc.dram_tensor("kwT", [HID, FS], F32, kind="ExternalInput")
    vwT = nc.dram_tensor("vwT", [HID, FS], F32, kind="ExternalInput")
    owT = nc.dram_tensor("owT", [HID, FS], F32, kind="ExternalInput")
    fw_nat = nc.dram_tensor("fw_nat", [FS, HID], F32, kind="ExternalInput")
    fbias = nc.dram_tensor("fbias", [1, FS], F32, kind="ExternalInput")
    cosT = nc.dram_tensor("cosT", [D, S], F32, kind="ExternalInput")
    sinT = nc.dram_tensor("sinT", [D, S], F32, kind="ExternalInput")
    masks = nc.dram_tensor("masks", [4, 128, 512], BF16, kind="ExternalInput")
    flag = nc.dram_tensor("flag", [1, 1], F32, kind="ExternalInput")
    y = nc.dram_tensor("y", [S, FS], F32, kind="ExternalOutput")

    xn_tok = nc.dram_tensor("xn_tok", [S, HID], BF16)
    xn_feat = nc.dram_tensor("xn_feat", [HID, S], BF16)
    qbuf = nc.dram_tensor("qbuf", [NH * D, S], BF16)
    kbuf = nc.dram_tensor("kbuf", [NH * D, S], BF16)
    vtok = nc.dram_tensor("vtok", [NH, 128, 32, D], BF16)
    att_mine = nc.dram_tensor("att_mine", [NH * D, S], BF16)
    att_full = nc.dram_tensor("att_full", [HID, S], BF16)
    wq_nat = nc.dram_tensor("wq_nat", [FS, HID], BF16)
    imp_in = nc.dram_tensor("imp_in", [1, 512], F32)
    imp_out = nc.dram_tensor("imp_out", [1, 512], F32)

    b = B(nc)
    A = nc.alloc_sbuf_tensor
    P = nc.alloc_psum_tensor

    def _cut1():
        dy = A("dummy_y2", [128, 512], F32)
        nvd = b.op("vector", "memset", dy[:], 0.0, inc=("vV", 1))
        b.wait("sync", "vV", nvd)
        b.dma("sync", "dY0", y[0:128, 0:512], dy[:])
        b.wait("sync", "dY0", b.ctr["dY0"])
        b.run_block()
        nc.finalize()
        return nc

    def _cut(k):
        if upto != k:
            return False
        dy = A("dummy_y", [128, 512], F32)
        nvd = b.op("vector", "memset", dy[:], 0.0, inc=("vV", 1))
        b.wait("sync", "vV", nvd)
        b.dma("sync", "dY0", y[0:128, 0:512], dy[:])
        b.wait("sync", "dY0", b.ctr["dY0"])
        b.run_block()
        nc.finalize()
        return True

    keepfull = A("keepfull", [128, 6], BF16)
    keepf32 = A("keepf32", [128, 6], F32)
    maxq2 = A("maxq2", [1, NH], F32)
    maxk2 = A("maxk2", [1, NH], F32)
    sbP, psP = nc.sbuf_base, nc.psum_base

    for s in (["dX0", "dX1", "dXnT0", "dXnT1", "dFlag"]
              + [f"dXtr{i}" for i in range(5)] + [f"dFeat{i}" for i in range(5)]
              + [
              "dCast", "dW", "dCS0", "dCS1", "dXC0", "dXC1", "dSp0", "dSp1", "dImp",
              "cC", "dKeep", "dQH0", "dQH1", "dKH0", "dKH1", "dVH0", "dVH1",
              "dAtt0", "dAtt1", "dWn0", "dWn1", "dWqn0", "dWqn1", "dWqT", "dOsb",
              "dAC0", "dAC1", "dXn30", "dXn31", "dXr0", "dXr1", "dY0", "dY1",
              "dBias", "vV", "aA", "pP", "gG", "bar"]):
        b.sem(s)

    x_sb = [A("x_sb0", [128, HID], F32), A("x_sb1", [128, HID], F32)]
    xn_sb = [A("xn_sb0", [128, HID], BF16), A("xn_sb1", [128, HID], BF16)]
    stt = [A("stt0", [128, 48], F32), A("stt1", [128, 48], F32)]
    agb = [A("ag0", [128, 2], F32), A("ag1", [128, 2], F32)]
    negmean = [A("negmean0", [128, 1], F32), A("negmean1", [128, 1], F32)]
    lnv = [A("lnv0", [128, 1], F32), A("lnv1", [128, 1], F32)]
    rstd = [A("rstd0", [128, 1], F32), A("rstd1", [128, 1], F32)]
    xtr = [A(f"xtr{i}", [128, 1024], BF16) for i in range(5)]
    epscol = A("epscol", [128, 1], F32)

    wsb = A("wsb", [128, 32 * 1024], BF16)
    xnc = [A("xnc0", [128, 32 * 512], BF16), A("xnc1", [128, 32 * 512], BF16)]
    coss = [A("coss0", [128, 512], F32)] * 2
    sins = [A("sins0", [128, 512], F32)] * 2
    rope1 = A("rope1", [128, 512], F32)
    rope2 = A("rope2", [128, 512], F32)
    rt = A("rt", [128, 512], F32)
    qout = [A("qout0", [128, 512], BF16), A("qout1", [128, 512], BF16)]
    sqn = [A("sqn0", [128, 512], BF16), A("sqn1", [128, 512], BF16),
           A("sqn2", [128, 512], BF16)]
    mx = A("mx", [1, 1], F32)
    qtmp = A("qtmp", [128, 1], F32)
    qbar = A("qbar", [128, NH], F32)
    qbar_bf = A("qbar_bf", [128, NH], BF16)
    ones128bf = A("ones128bf", [128, 1], BF16)
    onesrowA = A("onesrowA", [1, 128], F32)
    flag_sb = A("flag_sb", [1, 1], F32)
    imp_sb = A("imp_sb", [1, 512], F32)
    imp2_sb = A("imp2_sb", [1, 512], F32)
    impshift = A("impshift", [128, 5], F32)
    Bsb = rope2
    cmpscr = rope1
    rankc = A("rankc", [128, 1], F32)
    r512 = A("r512", [1, 1], F32)
    kvis = A("kvis", [128, 512], BF16)
    PRJ = [P("PRJ0", [128, 512], F32), P("PRJ1", [128, 512], F32),
           P("PRJ2", [128, 512], F32)]
    R1 = P("R1", [1, 512], F32)
    R2 = P("R2", [1, 512], F32)

    b.op("vector", "memset", ones128bf[:], 1.0)
    b.op("vector", "memset", onesrowA[:], 1.0)
    b.op("vector", "memset", qbar[:], 0.0)
    b.op("vector", "memset", impshift[:], 0.0)
    b.op("vector", "memset", maxq2[:], 0.0)
    b.op("vector", "memset", maxk2[:], 0.0, inc=("vV", 1))
    b.dma("sync", "dFlag", flag_sb[:], flag[:, :])

    def load_w(wdram):
        b.need("gpsimd", "wsb")
        nW = None
        for cc in range(4):
            nW = b.dma("gpsimd", "dW",
                       wsb[:].rearrange("p (c n) -> p c n", c=32)
                       [:, cc * 8:(cc + 1) * 8, :],
                       wdram.ap().rearrange("(c p) n -> p c n", p=128)
                       [:, cc * 8:(cc + 1) * 8, :])
        return nW

    def load_xnc(tc, want_cs):
        tp = tc % 2
        fd = feat_done[tc // 2]
        for i6 in range(5):
            b.wait("sync", f"dFeat{i6}", fd[i6])
        b.need("sync", f"xnc{tp}")
        nXC = None
        for cc in range(4):
            nXC = b.dma("sync", f"dXC{tp}",
                        xnc[tp][:].rearrange("p (c n) -> p c n", c=32)
                        [:, cc * 8:(cc + 1) * 8, :],
                        xn_feat[:, tc * 512:(tc + 1) * 512]
                        .rearrange("(c p) n -> p c n", p=128)
                        [:, cc * 8:(cc + 1) * 8, :])
        return nXC, None

    def qk_begin(wdram, kind):
        return {"normq": [], "prev_nvo": [0], "pend": {0: None, 1: None},
                "wready": load_w(wdram)}

    def qk_pass(wdram, kind, tc_list=None, st=None, flush=True):
        if tc_list is None:
            tc_list = list(range(DBG_NTC))
        if st is None:
            st = qk_begin(wdram, kind)
        normq = st["normq"]
        prev_nvo = st["prev_nvo"]

        def flush_norm(ent):
            sp_, h_, nsq_ = ent
            b.wait("tensor", "aA", nsq_)
            b.need("tensor", "R2")
            npn = b.op("tensor", "matmul", R2[:], ones128bf[:], sqn[sp_][:],
                       start=True, stop=True, inc=("pP", 1))
            b.released(f"sqn{sp_}", "pP", npn)
            b.wait("vector", "pP", npn)
            b.opw("vector", "reduce_max", mx[:], R2[:], AXX)
            tgt = maxq2 if kind == "q" else maxk2
            nvr = b.op("vector", "tensor_tensor", tgt[:, h_:h_ + 1],
                       tgt[:, h_:h_ + 1], mx[:], ALU.max, inc=("vV", 1))
            b.released("R2", "vV", nvr)

        wready = st["wready"]
        pend = st["pend"]
        for tc in tc_list:
            tp = tc % 2
            if pend.get(tp) is None or pend[tp][0] != tc:
                pend[tp] = (tc,) + load_xnc(tc, True)
            if tc + 1 < DBG_NTC and (tc + 1) // 2 in feat_done:
                pend[(tc + 1) % 2] = (tc + 1,) + load_xnc(tc + 1, True)
            _, nXC, _ = pend[tp]
            b.need("sync", "cs0")
            b.dma("sync", "dCS0", coss[0][:], cosT[:, tc * 512:(tc + 1) * 512])
            nCS = b.dma("sync", "dCS0", sins[0][:], sinT[:, tc * 512:(tc + 1) * 512])
            b.wait("tensor", f"dXC{tp}", nXC)
            if tc == 0:
                b.wait("tensor", "dW", wready)
            last_pmm = 0
            for h in range(DBG_NH):
                it = tc * NH + h
                pp = it % 3
                qp = it % 2
                sp = it % 3
                b.need("tensor", f"PRJ{pp}")
                for ic in range(32):
                    b.op("tensor", "matmul", PRJ[pp][:],
                         wsb[:, ic * 1024 + h * 128: ic * 1024 + h * 128 + 128],
                         xnc[tp][:, ic * 512:(ic + 1) * 512],
                         start=(ic == 0), stop=(ic == 31),
                         inc=("pP", 1) if ic == 31 else None)
                npmm = b.ctr["pP"]
                last_pmm = npmm
                if DBG_ROPE:
                    b.wait("vector", "pP", npmm)
                    b.wait("vector", "dCS0", nCS)
                    b.wait("vector", "vV", prev_nvo[0])
                    nv1 = b.op("vector", "tensor_tensor", rope1[:], PRJ[pp][:],
                               coss[tp][:], ALU.mult, inc=("vV", 1))
                    b.wait("scalar", "pP", npmm)
                    b.wait("scalar", "vV", nv1)
                    b.op("scalar", "activation", rt[0:64, :], PRJ[pp][64:128, :],
                         AF.Copy, scale=-1.0)
                    na1 = b.op("scalar", "activation", rt[64:128, :],
                               PRJ[pp][0:64, :], AF.Copy, inc=("aA", 1))
                    b.released(f"PRJ{pp}", "vV", nv1)
                    b.released(f"PRJ{pp}", "aA", na1)
                    b.wait("vector", "aA", na1)
                    b.opw("vector", "tensor_tensor", rope2[:], rt[:],
                          sins[tp][:], ALU.mult)
                    b.need("vector", f"qout{qp}")
                    nvo = b.op("vector", "tensor_tensor", qout[qp][:], rope1[:],
                               rope2[:], ALU.add, inc=("vV", 1))
                    prev_nvo[0] = nvo
                else:
                    b.wait("vector", "pP", npmm)
                    b.need("vector", f"qout{qp}")
                    nvo = b.op("vector", "tensor_copy", qout[qp][:], PRJ[pp][:],
                               inc=("vV", 1))
                    b.released(f"PRJ{pp}", "vV", nvo)
                if h == NH - 1:
                    b.released("cs0", "vV", nvo)
                if kind == "q" and DBG_QBAR:
                    lo = max(TEXT_LO, tc * 512) - tc * 512
                    hi = min(TEXT_HI, (tc + 1) * 512) - tc * 512
                    if hi > lo:
                        b.wait("vector", "vV", nvo)
                        b.opw("vector", "reduce_sum", qtmp[:],
                              qout[qp][:, lo:hi], AXX)
                        b.op("vector", "tensor_tensor", qbar[:, h:h + 1],
                             qbar[:, h:h + 1], qtmp[:], ALU.add)
                if DBG_NORMS:
                    b.need("scalar", f"sqn{sp}")
                    b.wait("scalar", "vV", nvo)
                    nsq = b.op("scalar", "activation", sqn[sp][:], qout[qp][:],
                               AF.Square, inc=("aA", 1))
                    b.released(f"qout{qp}", "aA", nsq)
                    normq.append((sp, h, nsq))
                    if len(normq) > 2:
                        flush_norm(normq.pop(0))
                b.wait("sync", "vV", nvo)
                dst = qbuf if kind == "q" else kbuf
                nst = b.dma("sync", f"dSp{qp}",
                            dst[h * 128:(h + 1) * 128, tc * 512:(tc + 1) * 512],
                            qout[qp][:])
                b.released(f"qout{qp}", f"dSp{qp}", nst)
            b.released(f"xnc{tp}", "pP", last_pmm)
        if flush:
            while normq:
                flush_norm(normq.pop(0))
            b.released("wsb", "pP", b.ctr["pP"])

    def v_pass(wdram):
        wready = load_w(wdram)
        pend = {0: None, 1: None}
        pend[0] = load_xnc(0, False)
        for tc in range(8):
            tp = tc % 2
            if tc + 1 < 8:
                pend[(tc + 1) % 2] = load_xnc(tc + 1, False)
            nXC, _ = pend[tp]
            b.wait("tensor", f"dXC{tp}", nXC)
            if tc == 0:
                b.wait("tensor", "dW", wready)
            for tt in range(4):
                for half in range(2):
                    it = tc * 8 + tt * 2 + half
                    pp = it % 3
                    qp = it % 2
                    b.need("tensor", f"PRJ{pp}")
                    for ic in range(32):
                        b.op("tensor", "matmul", PRJ[pp][:],
                             xnc[tp][:, ic * 512 + tt * 128: ic * 512 + tt * 128 + 128],
                             wsb[:, ic * 1024 + half * 512: ic * 1024 + half * 512 + 512],
                             start=(ic == 0), stop=(ic == 31),
                             inc=("pP", 1) if ic == 31 else None)
                    npmm = b.ctr["pP"]
                    b.wait("scalar", "pP", npmm)
                    b.need("scalar", f"qout{qp}")
                    nev = b.op("scalar", "activation", qout[qp][:], PRJ[pp][:],
                               AF.Copy, inc=("aA", 1))
                    b.released(f"PRJ{pp}", "aA", nev)
                    kc = tc * 4 + tt
                    b.wait("sync", "aA", nev)
                    nst = b.dma("sync", f"dSp{qp}",
                                vtok.ap()[half * 4:(half + 1) * 4, :, kc:kc + 1, :]
                                .rearrange("h p q d -> p h q d"),
                                qout[qp][:].rearrange("p (h q d) -> p h q d",
                                                      h=4, q=1))
                    b.released(f"qout{qp}", f"dSp{qp}", nst)
            b.released(f"xnc{tp}", "pP", b.ctr["pP"])
        b.released("wsb", "pP", b.ctr["pP"])

    feat_done = {}
    b.op("vector", "memset", epscol[:], 1e-5, inc=("vV", 1))
    qst = qk_begin(qwT, "q")
    nxs = [None] * 33
    nxs[0] = b.dma("sync", "dX0", x_sb[0][:], x_b[0:128, :])
    for i in range(32):
        par = i % 2
        if i + 1 < 32:
            np_ = (i + 1) % 2
            b.need("sync", f"x{np_}")
            nxs[i + 1] = b.dma("sync", f"dX{np_}", x_sb[np_][:],
                               x_b[(i + 1) * 128:(i + 2) * 128, :])
        b.wait("vector", f"dX{par}", nxs[i])
        for cc in range(8):
            nbs_ = b.op("vector", "bn_stats", stt[par][:, cc * 6:(cc + 1) * 6],
                        x_sb[par][:, cc * 512:(cc + 1) * 512],
                        inc=("vV", 1) if cc == 7 else None)
        b.wait("vector", "vV", nbs_)
        b.opw("vector", "bn_aggr", agb[par][:], stt[par][:])
        nv = b.op("vector", "tensor_scalar", negmean[par][:],
                  agb[par][:, 0:1], -1.0, None, ALU.mult, inc=("vV", 1))
        b.wait("scalar", "vV", nv)
        b.opw("scalar", "activation", lnv[par][:], agb[par][:, 1:2], AF.Ln,
              bias=epscol[:])
        na = b.op("scalar", "activation", rstd[par][:], lnv[par][:], AF.Exp,
                  scale=-0.5, inc=("aA", 1))
        b.wait("vector", "aA", na)
        b.need("vector", f"xn{par}")
        nv = b.op("vector", "tensor_scalar", xn_sb[par][:], x_sb[par][:],
                  negmean[par][:], rstd[par][:], ALU.add, ALU.mult,
                  inc=("vV", 1))
        b.released(f"x{par}", "vV", nv)
        b.wait("sync", "vV", nv)
        nst = b.dma("sync", f"dXnT{par}", xn_tok[i * 128:(i + 1) * 128, :],
                    xn_sb[par][:])
        b.released(f"xn{par}", f"dXnT{par}", nst)
        if i % 8 == 7:
            g = i // 8
            b.wait("scalar", "dXnT0", b.ctr["dXnT0"])
            b.wait("scalar", "dXnT1", b.ctr["dXnT1"])
            ntrs = [None] * 32
            for f in range(32):
                tp = f % 5
                b.need("scalar", f"xtr{tp}")
                ntrs[f] = b.dma("scalar", f"dXtr{tp}", xtr[tp][:],
                                xn_tok[g * 1024:(g + 1) * 1024,
                                       f * 128:(f + 1) * 128], transpose=True)
                if f >= 3:
                    fw = f - 3
                    wp = fw % 5
                    b.wait("scalar", f"dXtr{wp}", ntrs[fw])
                    nwr = b.dma("scalar", f"dFeat{wp}",
                                xn_feat[fw * 128:(fw + 1) * 128,
                                        g * 1024:(g + 1) * 1024], xtr[wp][:])
                    b.released(f"xtr{wp}", f"dFeat{wp}", nwr)
            for fw in range(29, 32):
                wp = fw % 5
                b.wait("scalar", f"dXtr{wp}", ntrs[fw])
                nwr = b.dma("scalar", f"dFeat{wp}",
                            xn_feat[fw * 128:(fw + 1) * 128,
                                    g * 1024:(g + 1) * 1024], xtr[wp][:])
                b.released(f"xtr{wp}", f"dFeat{wp}", nwr)
            feat_done[g] = tuple(b.ctr[f"dFeat{i}"] for i in range(5))
            if upto != 0:
                qk_pass(qwT, "q", [2 * g, 2 * g + 1], qst, flush=(g == 3))
    if upto == 0:
        for i6 in range(5):
            b.wait("sync", f"dFeat{i6}", b.ctr[f"dFeat{i6}"])
        nc.sbuf_base, nc.psum_base = sbP, psP
        b.run_block()
        _cut(0)
        return nc

    if sub == "q":
        return _cut1()
    qk_pass(kwT, "k")
    if sub == "qk":
        return _cut1()

    nvq = b.op("vector", "tensor_copy", qbar_bf[:], qbar[:], inc=("vV", 1))
    b.wait("sync", "dSp0", b.ctr["dSp0"])
    b.wait("sync", "dSp1", b.ctr["dSp1"])
    b.wait("tensor", "vV", nvq)
    for h in range(NH):
        b.need("sync", "kvis")
        nkv = b.dma("sync", "dKeep", kvis[:], kbuf[h * 128:(h + 1) * 128, 1:513])
        b.wait("tensor", "dKeep", nkv)
        npi = b.op("tensor", "matmul", R1[:], qbar_bf[:, h:h + 1], kvis[:],
                   start=(h == 0), stop=(h == NH - 1), inc=("pP", 1))
        b.released("kvis", "pP", npi)
    b.wait("vector", "pP", b.ctr["pP"])
    b.wait("vector", "dFlag", 16)
    nvi = b.op("vector", "tensor_scalar", imp_sb[:], R1[:], flag_sb[0:1, 0:1],
               None, ALU.mult, inc=("vV", 1))
    b.wait("sync", "vV", nvi)
    b.dma("sync", "dImp", imp_in[:, :], imp_sb[:])
    b.wait("gpsimd", "dImp", b.ctr["dImp"])
    b.op("gpsimd", "collective_compute", "AllReduce", ALU.add,
         replica_groups=[[0, 1, 2, 3, 4, 5, 6, 7]],
         ins=[imp_in.ap().opt()], outs=[imp_out.ap().opt()], inc=("cC", 1))

    if sub == "qki":
        return _cut1()
    v_pass(vwT)
    if sub == "qkiv":
        return _cut1()

    b.wait("sync", "cC", 1)
    b.wait("sync", "vV", 1)
    b.dma("sync", "dKeep", imp2_sb[:], imp_out[:, :])
    b.dma("sync", "dKeep", impshift[1:128, 0:1],
          imp_out[0, 0:127].rearrange("(r p) -> p r", p=127),
          allow_slow_non_contiguous=True)
    b.dma("sync", "dKeep", impshift[:, 1:4],
          imp_out[0, 127:511].rearrange("(r p) -> p r", p=128),
          allow_slow_non_contiguous=True)
    nks = b.dma("sync", "dKeep", impshift[0:1, 4:5],
                imp_out[0, 511:512].rearrange("(r p) -> p r", p=1),
                allow_slow_non_contiguous=True)
    b.wait("tensor", "dKeep", nks)
    b.need("tensor", "PRJ0")
    nbm = b.op("tensor", "matmul", PRJ[0][:], onesrowA[:], imp2_sb[:],
               start=True, stop=True, inc=("pP", 1))
    b.wait("scalar", "pP", nbm)
    nbs = b.op("scalar", "activation", Bsb[:], PRJ[0][:], AF.Copy, inc=("aA", 1))
    b.wait("vector", "aA", nbs)
    b.wait("vector", "dKeep", nks)
    b.wait("vector", "vV", nvi)
    for kc in range(4):
        b.opw("vector", "tensor_scalar", cmpscr[:], Bsb[:],
              impshift[:, kc:kc + 1], None, ALU.is_gt)
        b.opw("vector", "reduce_sum", rankc[:], cmpscr[:], AXX)
        b.op("vector", "tensor_scalar", keepf32[:, kc:kc + 1], rankc[:],
             255.5, None, ALU.is_lt)
    b.op("vector", "memset", keepf32[:, 4:6], 1.0)
    b.opw("vector", "tensor_scalar", cmpscr[0:1, :], Bsb[0:1, :],
          impshift[0:1, 4:5], None, ALU.is_gt)
    b.opw("vector", "reduce_sum", r512[:], cmpscr[0:1, :], AXX)
    b.op("vector", "tensor_scalar", keepf32[0:1, 4:5], r512[:],
         255.5, None, ALU.is_lt)
    b.opw("vector", "memset", keepf32[0:1, 0:1], 1.0)
    b.op("vector", "tensor_copy", keepfull[:], keepf32[:], inc=("vV", 1))
    b.wait("sync", "dSp0", b.ctr["dSp0"])
    b.wait("sync", "dSp1", b.ctr["dSp1"])
    for i6 in range(5):
        b.wait("sync", f"dFeat{i6}", b.ctr[f"dFeat{i6}"])
    nc.sbuf_base, nc.psum_base = sbP, psP
    b.run_block()
    if _cut(1):
        return nc

    b.start_block_waits()
    masks_sb = A("masks_sb", [128, 4 * 512], BF16)
    qh = [A("qh0", [128, S], BF16), A("qh1", [128, S], BF16)]
    Kh = [A("Kh0", [128, S], BF16), A("Kh1", [128, S], BF16)]
    Vh = [A("Vh0", [128, 32 * D], BF16), A("Vh1", [128, 32 * D], BF16)]
    pt = [A(f"pt{i}", [128, 512], BF16) for i in range(4)]
    rdb = [A("rdb0", [128, 512], F32), A("rdb1", [128, 512], F32)]
    attout = [A("attout0", [128, 512], BF16), A("attout1", [128, 512], BF16)]
    nm1 = A("nm1", [1, 1], F32)
    lnm = A("lnm", [1, 1], F32)
    ch = A("ch", [1, 1], F32)
    negc1 = A("negc1", [1, 1], F32)
    negcb = [A("negcb0", [128, 1], F32), A("negcb1", [128, 1], F32)]
    rcp = [A("rcp0", [1, 512], F32), A("rcp1", [1, 512], F32)]
    onesrow2 = A("onesrow2", [1, 128], F32)
    wnat = [A("wnat0", [128, HID], F32), A("wnat1", [128, HID], F32)]
    xfull = A("xfull", [128, HID], F32)
    sfull = A("sfull", [128, HID], F32)
    wqn = [A("wqn0", [128, HID], BF16), A("wqn1", [128, HID], BF16)]
    absm = A("absm", [128, 32], F32)
    sclq = A("sclq", [128, 32], F32)
    rinv = A("rinv", [128, 32], F32)
    gcol = A("gcol", [128, 1], F32)
    Spsum = [P(f"S{i}_", [128, 512], F32) for i in range(4)]
    CPV = [P("CPV0", [128, 512], F32), P("CPV1", [128, 512], F32)]
    DBC = P("DBC", [128, 512], F32)
    R1a = P("R1a", [1, 512], F32)
    NEG = DBC

    b.op("vector", "memset", onesrow2[:], 1.0, inc=("vV", 1))
    nmask = b.dma("sync", "dKeep", masks_sb[:].rearrange("p (o c) -> p o c", o=4),
                  masks.ap().rearrange("o p c -> p o c"))
    b.wait("vector", "dKeep", nmask)
    b.opw("vector", "memset", gcol[:], 1.0)
    b.op("vector", "tensor_scalar", gcol[:], gcol[:], C52, -C52, ALU.add,
         ALU.add, inc=("vV", 1))
    nwl = [None] * 8

    def load_wnat(ot):
        if ot < 8:
            b.need("sync", f"wnat{ot % 2}")
            nwl[ot] = b.dma("sync", f"dWn{ot % 2}", wnat[ot % 2][:],
                            fw_nat[ot * 128:(ot + 1) * 128, :])

    def quant_steps():
        load_wnat(0)
        load_wnat(1)
        for ot in range(8):
            par = ot % 2
            b.wait("vector", f"dWn{par}", nwl[ot])
            b.opw("vector", "tensor_reduce", absm[:],
                  wnat[par][:].rearrange("p (g c) -> p g c", c=128), AXX,
                  ALU.max, apply_absolute_value=True)
            b.opw("vector", "tensor_scalar", sclq[:], absm[:], 1.0 / 7.0, None,
                  ALU.mult)
            b.opw("vector", "tensor_scalar", absm[:], absm[:], 1e-30, None,
                  ALU.max)
            b.opw("vector", "reciprocal", rinv[:], absm[:])
            b.opw("vector", "tensor_scalar", rinv[:], rinv[:], 7.0, None,
                  ALU.mult)
            yield
            ngx = None
            for gb in range(4):
                for gi in range(8):
                    g = gb * 8 + gi
                    ngx = b.op("vector", "tensor_scalar",
                               xfull[:, g * 128:(g + 1) * 128],
                               wnat[par][:, g * 128:(g + 1) * 128],
                               rinv[:, g:g + 1], None, ALU.mult,
                               inc=("vV", 1) if g == 31 else None)
                yield
            b.wait("vector", "vV", ngx)
            nrel = b.op("vector", "tensor_scalar", xfull[:], xfull[:], C23,
                        -C23, ALU.add, ALU.add, inc=("vV", 1))
            b.released(f"wnat{par}", "vV", nrel)
            load_wnat(ot + 2)
            b.wait("vector", "vV", nrel)
            b.opw("vector", "tensor_scalar", sfull[:], xfull[:], C52, -C52,
                  ALU.add, ALU.add)
            b.opw("vector", "tensor_tensor", sfull[:], sfull[:], xfull[:],
                  ALU.subtract)
            b.opw("vector", "scalar_tensor_tensor", xfull[:], sfull[:],
                  gcol[:], xfull[:], ALU.mult, ALU.add)
            yield
            b.need("vector", f"wqn{par}")
            nq2 = None
            for gb in range(4):
                for gi in range(8):
                    g = gb * 8 + gi
                    nq2 = b.op("vector", "tensor_scalar",
                               wqn[par][:, g * 128:(g + 1) * 128],
                               xfull[:, g * 128:(g + 1) * 128],
                               sclq[:, g:g + 1], None, ALU.mult,
                               inc=("vV", 1) if g == 31 else None)
                yield
            b.wait("sync", "vV", nq2)
            nst = b.dma("sync", f"dWqn{par}",
                        wq_nat[ot * 128:(ot + 1) * 128, :], wqn[par][:])
            b.released(f"wqn{par}", f"dWqn{par}", nst)
            yield

    qgen = quant_steps()

    def att_loads(h):
        hp = h % 2
        b.need("sync", f"qh{hp}")
        nq = b.dma("sync", f"dQH{hp}", qh[hp][:], qbuf[h * 128:(h + 1) * 128, :])
        b.need("sync", f"Kh{hp}")
        nk = b.dma("sync", f"dKH{hp}", Kh[hp][:], kbuf[h * 128:(h + 1) * 128, :])
        b.need("sync", f"Vh{hp}")
        nv_ = b.dma("sync", f"dVH{hp}",
                    Vh[hp][:].rearrange("p (o kc d) -> p o kc d", o=1, kc=32),
                    vtok.ap()[h:h + 1, :, :, :].rearrange("o p kc d -> p o kc d"))
        return nq, nk, nv_

    pend = {0: att_loads(0), 1: None}
    pending_tail = []

    def emit_tail():
        while pending_tail:
            h_, j_, nrc_ = pending_tail.pop(0)
            jp_ = j_ % 2
            b.need("tensor", "DBC")
            b.wait("tensor", "vV", nrc_)
            nbc = b.op("tensor", "matmul", DBC[:], onesrow2[:], rcp[jp_][:],
                       start=True, stop=True, inc=("pP", 1))
            b.released(f"rcp{jp_}", "pP", nbc)
            b.wait("scalar", "pP", nbc)
            b.need("scalar", f"rdb{jp_}")
            nrd = b.op("scalar", "activation", rdb[jp_][:], DBC[:], AF.Copy,
                       inc=("aA", 1))
            b.released("DBC", "aA", nrd)
            b.wait("vector", "aA", nrd)
            b.need("vector", f"attout{jp_}")
            nao = b.op("vector", "tensor_tensor", attout[jp_][:], CPV[jp_][:],
                       rdb[jp_][:], ALU.mult, inc=("vV", 1))
            b.released(f"rdb{jp_}", "vV", nao)
            b.released(f"CPV{jp_}", "vV", nao)
            b.wait("sync", "vV", nao)
            nst = b.dma("sync", f"dAtt{jp_}",
                        att_mine[h_ * 128:(h_ + 1) * 128,
                                 j_ * 512:(j_ + 1) * 512],
                        attout[jp_][:])
            b.released(f"attout{jp_}", f"dAtt{jp_}", nst)

    for h in range(NH):
        hp = h % 2
        if h + 1 < NH:
            pend[(h + 1) % 2] = att_loads(h + 1)
        nq, nk, nv_ = pend[hp]
        b.opw("vector", "tensor_tensor", nm1[:], maxq2[:, h:h + 1],
              maxk2[:, h:h + 1], ALU.mult)
        nvn = b.op("vector", "tensor_scalar", nm1[:], nm1[:], 1e-30, None,
                   ALU.max, inc=("vV", 1))
        b.wait("scalar", "vV", nvn)
        b.opw("scalar", "activation", lnm[:], nm1[:], AF.Ln)
        b.opw("scalar", "activation", ch[:], lnm[:], AF.Exp, scale=0.5)
        nac = b.op("scalar", "mul", negc1[:], ch[:], -SCALE, inc=("aA", 1))
        b.wait("tensor", "aA", nac)
        b.need("tensor", "DBC")
        nnp = b.op("tensor", "matmul", NEG[:, 0:1], onesrow2[:], negc1[:],
                   start=True, stop=True, inc=("pP", 1))
        b.wait("scalar", "pP", nnp)
        nnc = b.op("scalar", "activation", negcb[hp][:], NEG[:, 0:1], AF.Copy,
                   inc=("aA", 1))
        b.wait("scalar", "aA", nnc)
        b.released("DBC", "aA", nnc)
        b.wait("vector", f"dVH{hp}", nv_)
        nvz = None
        for kc in range(5):
            nvz = b.op("vector", "tensor_scalar",
                       Vh[hp][:, kc * D:(kc + 1) * D],
                       Vh[hp][:, kc * D:(kc + 1) * D],
                       keepf32[:, kc:kc + 1], None, ALU.mult, inc=("vV", 1))
        b.wait("tensor", "vV", nvz)
        b.wait("tensor", f"dQH{hp}", nq)
        b.wait("tensor", f"dKH{hp}", nk)
        for j in range(8):
            kcn = 4 * j + 4
            jp = j % 2

            def emit_S(kc, hp=hp, j=j):
                sp_ = kc % 4
                b.need("tensor", f"S{sp_}")
                return b.op("tensor", "matmul", Spsum[sp_][:],
                            Kh[hp][:, kc * 128:(kc + 1) * 128],
                            qh[hp][:, j * 512:(j + 1) * 512],
                            start=True, stop=True, inc=("pP", 1))
            ns_of = {}
            for kc in range(min(4, kcn)):
                ns_of[kc] = emit_S(kc)
            emit_tail()
            b.need("tensor", "R1a")
            b.need("tensor", f"CPV{jp}")
            for kc in range(kcn):
                sp_ = kc % 4
                ns = ns_of.pop(kc)
                b.wait("scalar", "pP", ns)
                if j == 0 and kc == 0:
                    b.wait("scalar", "aA", nnc)
                b.need("scalar", f"pt{sp_}")
                nex = b.op("scalar", "activation", pt[sp_][:], Spsum[sp_][:],
                           AF.Exp, bias=negcb[hp][:], scale=SCALE,
                           inc=("aA", 1))
                b.released(f"S{sp_}", "aA", nex)
                gate = ("aA", nex)
                if kc >= 4 * j:
                    o = kc - 4 * j
                    b.wait("vector", "aA", nex)
                    nmk = b.op("vector", "tensor_tensor", pt[sp_][:], pt[sp_][:],
                               masks_sb[:, o * 512:(o + 1) * 512], ALU.mult,
                               inc=("vV", 1))
                    gate = ("vV", nmk)
                b.wait("tensor", gate[0], gate[1])
                kk = min(kc, 5)
                b.op("tensor", "matmul", R1a[:], keepfull[:, kk:kk + 1],
                     pt[sp_][:], start=(kc == 0), stop=(kc == kcn - 1))
                npv = b.op("tensor", "matmul", CPV[jp][:],
                           Vh[hp][:, kc * D:(kc + 1) * D], pt[sp_][:],
                           start=(kc == 0), stop=(kc == kcn - 1), inc=("pP", 1))
                b.released(f"pt{sp_}", "pP", npv)
                if kc + 4 < kcn:
                    ns_of[kc + 4] = emit_S(kc + 4)
            b.wait("vector", "pP", b.ctr["pP"])
            b.need("vector", f"rcp{jp}")
            nrc = b.op("vector", "reciprocal", rcp[jp][:], R1a[:], inc=("vV", 1))
            b.released("R1a", "vV", nrc)
            pending_tail.append((h, j, nrc))
            next(qgen, None)
        b.released(f"qh{hp}", "pP", b.ctr["pP"])
        b.released(f"Kh{hp}", "pP", b.ctr["pP"])
        b.released(f"Vh{hp}", "pP", b.ctr["pP"])
    emit_tail()
    for _ in qgen:
        pass
    b.wait("gpsimd", "dAtt0", b.ctr["dAtt0"])
    b.wait("gpsimd", "dAtt1", b.ctr["dAtt1"])
    b.op("gpsimd", "collective_compute", "AllGather", ALU.bypass,
         replica_groups=[[0, 1, 2, 3], [4, 5, 6, 7]],
         ins=[att_mine.ap().opt()], outs=[att_full.ap().opt()], inc=("cC", 1))
    nc.sbuf_base, nc.psum_base = sbP, psP
    b.run_block()
    if _cut(2):
        return nc

    b.start_block_waits()
    wqT_sb = A("wqT_sb", [128, 32 * 1024], BF16)
    sbP3, psP3 = nc.sbuf_base, nc.psum_base
    b.wait("sync", "dWqn0", b.ctr["dWqn0"])
    b.wait("sync", "dWqn1", b.ctr["dWqn1"])
    for ic in range(32):
        for oc in range(2):
            b.dma("sync", "dWqT",
                  wqT_sb[:, ic * 1024 + oc * 512: ic * 1024 + oc * 512 + 512],
                  wq_nat[oc * 512:(oc + 1) * 512, ic * 128:(ic + 1) * 128],
                  transpose=True)
    nc.sbuf_base, nc.psum_base = sbP3, psP3
    b.run_block()
    if _cut(3):
        return nc

    b.start_block_waits()
    osb = A("osb", [128, 32 * 1024], BF16)
    att_c = [A("att_c0", [128, 32 * 256], BF16), A("att_c1", [128, 32 * 256], BF16)]
    xnc3 = [A("xnc30", [128, 32 * 256], BF16), A("xnc31", [128, 32 * 256], BF16)]
    xresb = A("xresb", [128, 1024], F32)
    outb = [A("outb0", [128, 512], F32), A("outb1", [128, 512], F32)]
    biasv = A("biasv", [1, FS], F32)
    biasB = A("biasB", [128, FS], BF16)
    onesrow3 = A("onesrow3", [1, 128], F32)
    P3 = [P("P3A", [128, 512], F32), P("P3B", [128, 512], F32)]

    nv3 = b.op("vector", "memset", onesrow3[:], 1.0, inc=("vV", 1))
    b.dma("sync", "dBias", biasv[:], fbias[:, :])
    for cc in range(4):
        b.dma("gpsimd", "dOsb",
              osb[:].rearrange("p (c n) -> p c n", c=32)[:, cc * 8:(cc + 1) * 8, :],
              owT.ap().rearrange("(c p) n -> p c n", p=128)[:, cc * 8:(cc + 1) * 8, :])
    b.wait("sync", "cC", 2)
    b.wait("tensor", "dBias", b.ctr["dBias"])
    b.wait("tensor", "vV", nv3)
    for fo in range(2):
        nb1 = b.op("tensor", "matmul", P3[fo][:], onesrow3[:],
                   biasv[0:1, fo * 512:(fo + 1) * 512], start=True, stop=True,
                   inc=("pP", 1))
        b.wait("scalar", "pP", nb1)
        nb2 = b.op("scalar", "activation", biasB[:, fo * 512:(fo + 1) * 512],
                   P3[fo][:], AF.Copy, inc=("aA", 1))
        b.released(f"P3{fo}", "aA", nb2)
    b.wait("tensor", "dWqT", b.ctr["dWqT"])
    b.wait("tensor", "dOsb", b.ctr["dOsb"])

    def p3_loads(tc):
        tp = tc % 2
        t0 = tc * 256
        b.need("sync", f"att_c{tp}")
        nac_ = None
        for cc in range(2):
            nac_ = b.dma("sync", f"dAC{tp}",
                         att_c[tp][:].rearrange("p (c n) -> p c n", c=32)
                         [:, cc * 16:(cc + 1) * 16, :],
                         att_full[:, t0:t0 + 256]
                         .rearrange("(c p) n -> p c n", p=128)
                         [:, cc * 16:(cc + 1) * 16, :])
        b.need("sync", f"xnc3{tp}")
        nxc3 = None
        for cc in range(2):
            nxc3 = b.dma("sync", f"dXn3{tp}",
                         xnc3[tp][:].rearrange("p (c n) -> p c n", c=32)
                         [:, cc * 16:(cc + 1) * 16, :],
                         xn_feat[:, t0:t0 + 256]
                         .rearrange("(c p) n -> p c n", p=128)
                         [:, cc * 16:(cc + 1) * 16, :])
        return nac_, nxc3

    pend3 = {0: p3_loads(0), 1: None}
    for tc in range(16):
        tp = tc % 2
        t0 = tc * 256
        if tc + 1 < 16:
            pend3[(tc + 1) % 2] = p3_loads(tc + 1)
        nac_, nxc3 = pend3[tp]
        b.wait("tensor", f"dAC{tp}", nac_)
        b.wait("tensor", f"dXn3{tp}", nxc3)
        for tt in range(2):
            trow = t0 + tt * 128
            b.need("sync", "xres")
            nxr = b.dma("sync", "dXr0", xresb[:],
                        xres_i[trow:trow + 128, :])
            b.wait("vector", "dXr0", nxr)
            for fo in range(2):
                pp = (tc * 4 + tt * 2 + fo) % 2
                op_ = pp
                b.need("tensor", f"P3{pp}")
                for hc in range(32):
                    b.op("tensor", "matmul", P3[pp][:],
                         att_c[tp][:, hc * 256 + tt * 128: hc * 256 + tt * 128 + 128],
                         osb[:, hc * 1024 + fo * 512: hc * 1024 + fo * 512 + 512],
                         start=(hc == 0), stop=False)
                for ic in range(32):
                    b.op("tensor", "matmul", P3[pp][:],
                         xnc3[tp][:, ic * 256 + tt * 128: ic * 256 + tt * 128 + 128],
                         wqT_sb[:, ic * 1024 + fo * 512: ic * 1024 + fo * 512 + 512],
                         start=False, stop=(ic == 31),
                         inc=("pP", 1) if ic == 31 else None)
                npf = b.ctr["pP"]
                b.wait("vector", "pP", npf)
                b.need("vector", f"outb{op_}")
                b.opw("vector", "tensor_tensor", outb[op_][:], P3[pp][:],
                      biasB[:, fo * 512:(fo + 1) * 512], ALU.add)
                nvo = b.op("vector", "tensor_tensor", outb[op_][:], outb[op_][:],
                           xresb[:, fo * 512:(fo + 1) * 512], ALU.add,
                           inc=("vV", 1))
                b.released(f"P3{pp}", "vV", nvo)
                if fo == 1:
                    b.released("xres", "vV", nvo)
                b.wait("sync", "vV", nvo)
                nyw = b.dma("sync", f"dY{op_}",
                            y[trow:trow + 128, fo * 512:(fo + 1) * 512],
                            outb[op_][:])
                b.released(f"outb{op_}", f"dY{op_}", nyw)
        b.released(f"att_c{tp}", "pP", b.ctr["pP"])
        b.released(f"xnc3{tp}", "pP", b.ctr["pP"])
    b.wait("sync", "dY0", b.ctr["dY0"])
    b.wait("sync", "dY1", b.ctr["dY1"])
    b.run_block()

    nc.finalize()
    return nc


_NC_CACHE = []


def _host_tables():
    inv = (10000.0 ** (-(np.arange(64, dtype=np.float64) / 64.0)))
    t = np.arange(S, dtype=np.float64)
    ang = t[None, :] * inv[:, None]
    ang = np.concatenate([ang, ang], axis=0)
    cosTv = np.cos(ang).astype(np.float32)
    sinTv = np.sin(ang).astype(np.float32)
    p = np.arange(128)[:, None]
    c = np.arange(512)[None, :]
    m = np.stack([(p + 128 * o <= c) for o in range(4)], 0)
    masksv = m.astype(ml_dtypes.bfloat16)
    return cosTv, sinTv, masksv


def kernel(L0_attn_hidden_states, L0_attn_q_w, L0_attn_k_w, L0_attn_v_w,
           L0_attn_o_w, L0_ffn_weight, L0_ffn_bias):
    x = np.asarray(L0_attn_hidden_states)
    qwTv = np.ascontiguousarray(np.asarray(L0_attn_q_w).T)
    kwTv = np.ascontiguousarray(np.asarray(L0_attn_k_w).T)
    vwTv = np.ascontiguousarray(np.asarray(L0_attn_v_w).T)
    owTv = np.ascontiguousarray(np.asarray(L0_attn_o_w).T)
    fw = np.asarray(L0_ffn_weight)
    fb = np.asarray(L0_ffn_bias)
    cosTv, sinTv, masksv = _host_tables()

    if not _NC_CACHE:
        _NC_CACHE.append(build())
    nc = _NC_CACHE[0]

    in_maps = []
    for c in range(8):
        bidx, r = c // 4, c % 4
        fs = slice(r * FS, (r + 1) * FS)
        in_maps.append({
            "x_b": np.ascontiguousarray(x[bidx]),
            "xres_i": np.ascontiguousarray(x[bidx][:, fs]),
            "qwT": np.ascontiguousarray(qwTv[:, fs]),
            "kwT": np.ascontiguousarray(kwTv[:, fs]),
            "vwT": np.ascontiguousarray(vwTv[:, fs]),
            "owT": np.ascontiguousarray(owTv[:, fs]),
            "fw_nat": np.ascontiguousarray(fw[fs, :]),
            "fbias": np.ascontiguousarray(fb[fs][None, :]),
            "cosT": cosTv, "sinT": sinTv, "masks": masksv,
            "flag": np.array([[1.0 if bidx == 0 else 0.0]], np.float32),
        })
    res = run_bass_kernel_spmd(nc, in_maps, core_ids=list(range(8)))
    out = np.empty((2, S, HID), np.float32)
    for c in range(8):
        bidx, r = c // 4, c % 4
        out[bidx][:, r * FS:(r + 1) * FS] = np.asarray(res.results[c]["y"])
    return out


# version 12
# speedup vs baseline: 1.3708x; 1.0222x over previous
import numpy as np
import ml_dtypes
import concourse.bass as bass
import concourse.mybir as mybir
from concourse.bass_utils import run_bass_kernel_spmd

F32 = mybir.dt.float32
BF16 = mybir.dt.bfloat16
AF = mybir.ActivationFunctionType
ALU = mybir.AluOpType
AXX = mybir.AxisListType.X

S = 4096
HID = 4096
D = 128
NH = 8
FS = 1024
SCALE = 0.08838834764831845
C23 = float(np.float32(1.5 * 2.0 ** 23))
C52 = float(np.float32(1.5 * 2.0 ** 52))
TEXT_LO, TEXT_HI = 513, 4040
DBG_NTC = 8
DBG_NH = 8
DBG_NORMS = True
DBG_QBAR = True
DBG_ROPE = True


class B:
    ENGS = ("sync", "gpsimd", "vector", "scalar", "tensor")

    def __init__(self, nc):
        self.nc = nc
        self.prog = {e: [] for e in self.ENGS}
        self.sems = {}
        self.ctr = {}
        self.rel = {}
        self.nbar = 0

    def sem(self, name):
        if name not in self.sems:
            self.sems[name] = self.nc.alloc_semaphore(name=name)
            self.ctr[name] = 0
        return self.sems[name]

    def op(self, eng, meth, *args, inc=None, **kw):
        s = self.sems[inc[0]] if inc else None
        if inc:
            self.ctr[inc[0]] += inc[1]

        def fn(e, meth=meth, args=args, kw=kw, s=s, iv=inc[1] if inc else None):
            ins = getattr(e, meth)(*args, **kw)
            if s is not None:
                ins.then_inc(s, iv)
            return ins
        self.prog[eng].append(fn)
        return self.ctr[inc[0]] if inc else None

    def opw(self, eng, meth, *args, **kw):
        sem = {"vector": "vV", "scalar": "aA", "tensor": "pP",
               "gpsimd": "gG"}[eng]
        v = self.op(eng, meth, *args, inc=(sem, 1), **kw)
        self.wait(eng, sem, v)
        return v

    def wait(self, eng, semname, val):
        if val is None or val <= 0:
            return
        s = self.sems[semname]
        self.prog[eng].append(lambda e, s=s, v=val: e.wait_ge(s, v))

    def dma(self, eng, semname, out, in_, **kw):
        self.sem(semname)
        return self.op(eng, "dma_start", out, in_, inc=(semname, 16), **kw)

    def released(self, key, semname, val):
        self.rel.setdefault(key, []).append((semname, val))

    def need(self, eng, key):
        for (s, v) in self.rel.pop(key, []):
            self.wait(eng, s, v)

    def run_block(self):
        self.sem("bar")
        self.nbar += 1
        for e in self.ENGS:
            if e != "sync":
                self.prog[e].append(lambda E: E.drain())
            s = self.sems["bar"]
            self.prog[e].append(lambda E, s=s: E.sem_inc(s, 1))
        nc = self.nc
        with nc.Block() as block:
            for ename, reg in (("sync", block.sync), ("gpsimd", block.gpsimd),
                               ("vector", block.vector), ("scalar", block.scalar),
                               ("tensor", block.tensor)):
                lst = self.prog[ename]

                def body(E, lst=lst):
                    for fn in lst:
                        fn(E)
                reg(body)
        self.prog = {e: [] for e in self.ENGS}

    def start_block_waits(self):
        for e in self.ENGS:
            self.wait(e, "bar", 5 * self.nbar)


def build(upto=4, sub=None):
    nc = bass.Bass("TRN2", target_bir_lowering=False)

    x_b = nc.dram_tensor("x_b", [S, HID], F32, kind="ExternalInput")
    xres_i = nc.dram_tensor("xres_i", [S, FS], F32, kind="ExternalInput")
    qwT = nc.dram_tensor("qwT", [HID, FS], F32, kind="ExternalInput")
    kwT = nc.dram_tensor("kwT", [HID, FS], F32, kind="ExternalInput")
    vwT = nc.dram_tensor("vwT", [HID, FS], F32, kind="ExternalInput")
    owT = nc.dram_tensor("owT", [HID, FS], F32, kind="ExternalInput")
    fw_nat = nc.dram_tensor("fw_nat", [FS, HID], F32, kind="ExternalInput")
    fbias = nc.dram_tensor("fbias", [1, FS], F32, kind="ExternalInput")
    cosT = nc.dram_tensor("cosT", [D, S], F32, kind="ExternalInput")
    sinT = nc.dram_tensor("sinT", [D, S], F32, kind="ExternalInput")
    masks = nc.dram_tensor("masks", [4, 128, 512], BF16, kind="ExternalInput")
    flag = nc.dram_tensor("flag", [1, 1], F32, kind="ExternalInput")
    y = nc.dram_tensor("y", [S, FS], F32, kind="ExternalOutput")

    xn_tok = nc.dram_tensor("xn_tok", [S, HID], BF16)
    xn_feat = nc.dram_tensor("xn_feat", [HID, S], BF16)
    qbuf = nc.dram_tensor("qbuf", [NH * D, S], BF16)
    kbuf = nc.dram_tensor("kbuf", [NH * D, S], BF16)
    vtok = nc.dram_tensor("vtok", [NH, 128, 32, D], BF16)
    att_mine = nc.dram_tensor("att_mine", [NH * D, S], BF16)
    att_full = nc.dram_tensor("att_full", [HID, S], BF16)
    wq_nat = nc.dram_tensor("wq_nat", [FS, HID], BF16)
    imp_in = nc.dram_tensor("imp_in", [1, 512], F32)
    imp_out = nc.dram_tensor("imp_out", [1, 512], F32)

    b = B(nc)
    A = nc.alloc_sbuf_tensor
    P = nc.alloc_psum_tensor

    def _cut1():
        dy = A("dummy_y2", [128, 512], F32)
        nvd = b.op("vector", "memset", dy[:], 0.0, inc=("vV", 1))
        b.wait("sync", "vV", nvd)
        b.dma("sync", "dY0", y[0:128, 0:512], dy[:])
        b.wait("sync", "dY0", b.ctr["dY0"])
        b.run_block()
        nc.finalize()
        return nc

    def _cut(k):
        if upto != k:
            return False
        dy = A("dummy_y", [128, 512], F32)
        nvd = b.op("vector", "memset", dy[:], 0.0, inc=("vV", 1))
        b.wait("sync", "vV", nvd)
        b.dma("sync", "dY0", y[0:128, 0:512], dy[:])
        b.wait("sync", "dY0", b.ctr["dY0"])
        b.run_block()
        nc.finalize()
        return True

    keepfull = A("keepfull", [128, 6], BF16)
    keepf32 = A("keepf32", [128, 6], F32)
    maxq2 = A("maxq2", [1, NH], F32)
    maxk2 = A("maxk2", [1, NH], F32)
    sbP, psP = nc.sbuf_base, nc.psum_base

    for s in (["dX0", "dX1", "dXnT0", "dXnT1", "dFlag"]
              + [f"dXtr{i}" for i in range(5)] + [f"dFeat{i}" for i in range(5)]
              + [
              "dCast", "dW", "dCS0", "dCS1", "dXC0", "dXC1", "dSp0", "dSp1", "dImp",
              "cC", "dKeep", "dQH0", "dQH1", "dKH0", "dKH1", "dVH0", "dVH1",
              "dAtt0", "dAtt1", "dWn0", "dWn1", "dWqn0", "dWqn1", "dWqT", "dOsb",
              "dAC0", "dAC1", "dXn30", "dXn31", "dXr0", "dXr1", "dY0", "dY1",
              "dBias", "vV", "aA", "pP", "gG", "bar"]):
        b.sem(s)

    x_sb = [A("x_sb0", [128, HID], F32), A("x_sb1", [128, HID], F32)]
    xn_sb = [A("xn_sb0", [128, HID], BF16), A("xn_sb1", [128, HID], BF16)]
    stt = [A("stt0", [128, 48], F32), A("stt1", [128, 48], F32)]
    agb = [A("ag0", [128, 2], F32), A("ag1", [128, 2], F32)]
    negmean = [A("negmean0", [128, 1], F32), A("negmean1", [128, 1], F32)]
    lnv = [A("lnv0", [128, 1], F32), A("lnv1", [128, 1], F32)]
    rstd = [A("rstd0", [128, 1], F32), A("rstd1", [128, 1], F32)]
    xtr = [A(f"xtr{i}", [128, 1024], BF16) for i in range(5)]
    epscol = A("epscol", [128, 1], F32)

    wsb = A("wsb", [128, 32 * 1024], BF16)
    xnc = [A("xnc0", [128, 32 * 512], BF16), A("xnc1", [128, 32 * 512], BF16)]
    coss = [A("coss0", [128, 512], F32)] * 2
    sins = [A("sins0", [128, 512], F32)] * 2
    rope1 = A("rope1", [128, 512], F32)
    rope2 = A("rope2", [128, 512], F32)
    rt = A("rt", [128, 512], F32)
    qout = [A("qout0", [128, 512], BF16), A("qout1", [128, 512], BF16)]
    sqn = [A("sqn0", [128, 512], BF16), A("sqn1", [128, 512], BF16),
           A("sqn2", [128, 512], BF16)]
    mx = A("mx", [1, 1], F32)
    qtmp = A("qtmp", [128, 1], F32)
    qbar = A("qbar", [128, NH], F32)
    qbar_bf = A("qbar_bf", [128, NH], BF16)
    ones128bf = A("ones128bf", [128, 1], BF16)
    onesrowA = A("onesrowA", [1, 128], F32)
    flag_sb = A("flag_sb", [1, 1], F32)
    imp_sb = A("imp_sb", [1, 512], F32)
    imp2_sb = A("imp2_sb", [1, 512], F32)
    impshift = A("impshift", [128, 5], F32)
    Bsb = rope2
    cmpscr = rope1
    rankc = A("rankc", [128, 1], F32)
    r512 = A("r512", [1, 1], F32)
    kvis = A("kvis", [128, 512], BF16)
    PRJ = [P("PRJ0", [128, 512], F32), P("PRJ1", [128, 512], F32),
           P("PRJ2", [128, 512], F32)]
    R1 = P("R1", [1, 512], F32)
    R2 = P("R2", [1, 512], F32)

    b.op("vector", "memset", ones128bf[:], 1.0)
    b.op("vector", "memset", onesrowA[:], 1.0)
    b.op("vector", "memset", qbar[:], 0.0)
    b.op("vector", "memset", impshift[:], 0.0)
    b.op("vector", "memset", maxq2[:], 0.0)
    b.op("vector", "memset", maxk2[:], 0.0, inc=("vV", 1))
    b.dma("sync", "dFlag", flag_sb[:], flag[:, :])

    def load_w(wdram):
        b.need("gpsimd", "wsb")
        nW = None
        for cc in range(4):
            nW = b.dma("gpsimd", "dW",
                       wsb[:].rearrange("p (c n) -> p c n", c=32)
                       [:, cc * 8:(cc + 1) * 8, :],
                       wdram.ap().rearrange("(c p) n -> p c n", p=128)
                       [:, cc * 8:(cc + 1) * 8, :])
        return nW

    def load_xnc(tc, want_cs):
        tp = tc % 2
        fd = feat_done[tc // 2]
        for i6 in range(5):
            b.wait("sync", f"dFeat{i6}", fd[i6])
        b.need("sync", f"xnc{tp}")
        nXC = None
        for cc in range(4):
            nXC = b.dma("sync", f"dXC{tp}",
                        xnc[tp][:].rearrange("p (c n) -> p c n", c=32)
                        [:, cc * 8:(cc + 1) * 8, :],
                        xn_feat[:, tc * 512:(tc + 1) * 512]
                        .rearrange("(c p) n -> p c n", p=128)
                        [:, cc * 8:(cc + 1) * 8, :])
        return nXC, None

    def qk_begin(wdram, kind):
        return {"normq": [], "prev_nvo": [0], "pend": {0: None, 1: None},
                "wready": load_w(wdram)}

    def qk_pass(wdram, kind, tc_list=None, st=None, flush=True):
        if tc_list is None:
            tc_list = list(range(DBG_NTC))
        if st is None:
            st = qk_begin(wdram, kind)
        normq = st["normq"]
        prev_nvo = st["prev_nvo"]

        def flush_norm(ent):
            sp_, h_, nsq_ = ent
            b.wait("tensor", "aA", nsq_)
            b.need("tensor", "R2")
            npn = b.op("tensor", "matmul", R2[:], ones128bf[:], sqn[sp_][:],
                       start=True, stop=True, inc=("pP", 1))
            b.released(f"sqn{sp_}", "pP", npn)
            b.wait("vector", "pP", npn)
            b.opw("vector", "reduce_max", mx[:], R2[:], AXX)
            tgt = maxq2 if kind == "q" else maxk2
            nvr = b.op("vector", "tensor_tensor", tgt[:, h_:h_ + 1],
                       tgt[:, h_:h_ + 1], mx[:], ALU.max, inc=("vV", 1))
            b.released("R2", "vV", nvr)

        wready = st["wready"]
        pend = st["pend"]
        for tc in tc_list:
            tp = tc % 2
            if pend.get(tp) is None or pend[tp][0] != tc:
                pend[tp] = (tc,) + load_xnc(tc, True)
            if tc + 1 < DBG_NTC and (tc + 1) // 2 in feat_done:
                pend[(tc + 1) % 2] = (tc + 1,) + load_xnc(tc + 1, True)
            _, nXC, _ = pend[tp]
            b.need("sync", "cs0")
            b.dma("sync", "dCS0", coss[0][:], cosT[:, tc * 512:(tc + 1) * 512])
            nCS = b.dma("sync", "dCS0", sins[0][:], sinT[:, tc * 512:(tc + 1) * 512])
            b.wait("tensor", f"dXC{tp}", nXC)
            if tc == 0:
                b.wait("tensor", "dW", wready)
            last_pmm = 0
            for h in range(DBG_NH):
                it = tc * NH + h
                pp = it % 3
                qp = it % 2
                sp = it % 3
                b.need("tensor", f"PRJ{pp}")
                for ic in range(32):
                    b.op("tensor", "matmul", PRJ[pp][:],
                         wsb[:, ic * 1024 + h * 128: ic * 1024 + h * 128 + 128],
                         xnc[tp][:, ic * 512:(ic + 1) * 512],
                         start=(ic == 0), stop=(ic == 31),
                         inc=("pP", 1) if ic == 31 else None)
                npmm = b.ctr["pP"]
                last_pmm = npmm
                if DBG_ROPE:
                    b.wait("vector", "pP", npmm)
                    b.wait("vector", "dCS0", nCS)
                    b.wait("vector", "vV", prev_nvo[0])
                    nv1 = b.op("vector", "tensor_tensor", rope1[:], PRJ[pp][:],
                               coss[tp][:], ALU.mult, inc=("vV", 1))
                    b.wait("scalar", "pP", npmm)
                    b.wait("scalar", "vV", nv1)
                    b.op("scalar", "activation", rt[0:64, :], PRJ[pp][64:128, :],
                         AF.Copy, scale=-1.0)
                    na1 = b.op("scalar", "activation", rt[64:128, :],
                               PRJ[pp][0:64, :], AF.Copy, inc=("aA", 1))
                    b.released(f"PRJ{pp}", "vV", nv1)
                    b.released(f"PRJ{pp}", "aA", na1)
                    b.wait("vector", "aA", na1)
                    b.opw("vector", "tensor_tensor", rope2[:], rt[:],
                          sins[tp][:], ALU.mult)
                    b.need("vector", f"qout{qp}")
                    nvo = b.op("vector", "tensor_tensor", qout[qp][:], rope1[:],
                               rope2[:], ALU.add, inc=("vV", 1))
                    prev_nvo[0] = nvo
                else:
                    b.wait("vector", "pP", npmm)
                    b.need("vector", f"qout{qp}")
                    nvo = b.op("vector", "tensor_copy", qout[qp][:], PRJ[pp][:],
                               inc=("vV", 1))
                    b.released(f"PRJ{pp}", "vV", nvo)
                if h == NH - 1:
                    b.released("cs0", "vV", nvo)
                if kind == "q" and DBG_QBAR:
                    lo = max(TEXT_LO, tc * 512) - tc * 512
                    hi = min(TEXT_HI, (tc + 1) * 512) - tc * 512
                    if hi > lo:
                        b.wait("vector", "vV", nvo)
                        b.opw("vector", "reduce_sum", qtmp[:],
                              qout[qp][:, lo:hi], AXX)
                        b.op("vector", "tensor_tensor", qbar[:, h:h + 1],
                             qbar[:, h:h + 1], qtmp[:], ALU.add)
                if DBG_NORMS:
                    b.need("scalar", f"sqn{sp}")
                    b.wait("scalar", "vV", nvo)
                    nsq = b.op("scalar", "activation", sqn[sp][:], qout[qp][:],
                               AF.Square, inc=("aA", 1))
                    b.released(f"qout{qp}", "aA", nsq)
                    normq.append((sp, h, nsq))
                    if len(normq) > 2:
                        flush_norm(normq.pop(0))
                b.wait("sync", "vV", nvo)
                dst = qbuf if kind == "q" else kbuf
                nst = b.dma("sync", f"dSp{qp}",
                            dst[h * 128:(h + 1) * 128, tc * 512:(tc + 1) * 512],
                            qout[qp][:])
                b.released(f"qout{qp}", f"dSp{qp}", nst)
            b.released(f"xnc{tp}", "pP", last_pmm)
        if flush:
            while normq:
                flush_norm(normq.pop(0))
            b.released("wsb", "pP", b.ctr["pP"])

    def v_pass(wdram):
        wready = load_w(wdram)
        pend = {0: None, 1: None}
        pend[0] = load_xnc(0, False)
        for tc in range(8):
            tp = tc % 2
            if tc + 1 < 8:
                pend[(tc + 1) % 2] = load_xnc(tc + 1, False)
            nXC, _ = pend[tp]
            b.wait("tensor", f"dXC{tp}", nXC)
            if tc == 0:
                b.wait("tensor", "dW", wready)
            for tt in range(4):
                for half in range(2):
                    it = tc * 8 + tt * 2 + half
                    pp = it % 3
                    qp = it % 2
                    b.need("tensor", f"PRJ{pp}")
                    for ic in range(32):
                        b.op("tensor", "matmul", PRJ[pp][:],
                             xnc[tp][:, ic * 512 + tt * 128: ic * 512 + tt * 128 + 128],
                             wsb[:, ic * 1024 + half * 512: ic * 1024 + half * 512 + 512],
                             start=(ic == 0), stop=(ic == 31),
                             inc=("pP", 1) if ic == 31 else None)
                    npmm = b.ctr["pP"]
                    b.wait("scalar", "pP", npmm)
                    b.need("scalar", f"qout{qp}")
                    nev = b.op("scalar", "activation", qout[qp][:], PRJ[pp][:],
                               AF.Copy, inc=("aA", 1))
                    b.released(f"PRJ{pp}", "aA", nev)
                    kc = tc * 4 + tt
                    b.wait("sync", "aA", nev)
                    nst = b.dma("sync", f"dSp{qp}",
                                vtok.ap()[half * 4:(half + 1) * 4, :, kc:kc + 1, :]
                                .rearrange("h p q d -> p h q d"),
                                qout[qp][:].rearrange("p (h q d) -> p h q d",
                                                      h=4, q=1))
                    b.released(f"qout{qp}", f"dSp{qp}", nst)
            b.released(f"xnc{tp}", "pP", b.ctr["pP"])
        b.released("wsb", "pP", b.ctr["pP"])

    feat_done = {}
    b.op("vector", "memset", epscol[:], 1e-5, inc=("vV", 1))
    qst = qk_begin(qwT, "q")
    nxs = [None] * 33
    nxs[0] = b.dma("sync", "dX0", x_sb[0][:], x_b[0:128, :])
    for i in range(32):
        par = i % 2
        if i + 1 < 32:
            np_ = (i + 1) % 2
            b.need("sync", f"x{np_}")
            nxs[i + 1] = b.dma("sync", f"dX{np_}", x_sb[np_][:],
                               x_b[(i + 1) * 128:(i + 2) * 128, :])
        b.wait("vector", f"dX{par}", nxs[i])
        for cc in range(8):
            nbs_ = b.op("vector", "bn_stats", stt[par][:, cc * 6:(cc + 1) * 6],
                        x_sb[par][:, cc * 512:(cc + 1) * 512],
                        inc=("vV", 1) if cc == 7 else None)
        b.wait("vector", "vV", nbs_)
        b.opw("vector", "bn_aggr", agb[par][:], stt[par][:])
        nv = b.op("vector", "tensor_scalar", negmean[par][:],
                  agb[par][:, 0:1], -1.0, None, ALU.mult, inc=("vV", 1))
        b.wait("scalar", "vV", nv)
        b.opw("scalar", "activation", lnv[par][:], agb[par][:, 1:2], AF.Ln,
              bias=epscol[:])
        na = b.op("scalar", "activation", rstd[par][:], lnv[par][:], AF.Exp,
                  scale=-0.5, inc=("aA", 1))
        b.wait("vector", "aA", na)
        b.need("vector", f"xn{par}")
        nv = b.op("vector", "tensor_scalar", xn_sb[par][:], x_sb[par][:],
                  negmean[par][:], rstd[par][:], ALU.add, ALU.mult,
                  inc=("vV", 1))
        b.released(f"x{par}", "vV", nv)
        b.wait("sync", "vV", nv)
        nst = b.dma("sync", f"dXnT{par}", xn_tok[i * 128:(i + 1) * 128, :],
                    xn_sb[par][:])
        b.released(f"xn{par}", f"dXnT{par}", nst)
        if i % 8 == 7:
            g = i // 8
            b.wait("scalar", "dXnT0", b.ctr["dXnT0"])
            b.wait("scalar", "dXnT1", b.ctr["dXnT1"])
            ntrs = [None] * 32
            for f in range(32):
                tp = f % 5
                b.need("scalar", f"xtr{tp}")
                ntrs[f] = b.dma("scalar", f"dXtr{tp}", xtr[tp][:],
                                xn_tok[g * 1024:(g + 1) * 1024,
                                       f * 128:(f + 1) * 128], transpose=True)
                if f >= 3:
                    fw = f - 3
                    wp = fw % 5
                    b.wait("scalar", f"dXtr{wp}", ntrs[fw])
                    nwr = b.dma("scalar", f"dFeat{wp}",
                                xn_feat[fw * 128:(fw + 1) * 128,
                                        g * 1024:(g + 1) * 1024], xtr[wp][:])
                    b.released(f"xtr{wp}", f"dFeat{wp}", nwr)
            for fw in range(29, 32):
                wp = fw % 5
                b.wait("scalar", f"dXtr{wp}", ntrs[fw])
                nwr = b.dma("scalar", f"dFeat{wp}",
                            xn_feat[fw * 128:(fw + 1) * 128,
                                    g * 1024:(g + 1) * 1024], xtr[wp][:])
                b.released(f"xtr{wp}", f"dFeat{wp}", nwr)
            feat_done[g] = tuple(b.ctr[f"dFeat{i}"] for i in range(5))
            if upto != 0:
                qk_pass(qwT, "q", [2 * g, 2 * g + 1], qst, flush=(g == 3))
    if upto == 0:
        for i6 in range(5):
            b.wait("sync", f"dFeat{i6}", b.ctr[f"dFeat{i6}"])
        nc.sbuf_base, nc.psum_base = sbP, psP
        b.run_block()
        _cut(0)
        return nc

    if sub == "q":
        return _cut1()
    qk_pass(kwT, "k")
    if sub == "qk":
        return _cut1()

    nvq = b.op("vector", "tensor_copy", qbar_bf[:], qbar[:], inc=("vV", 1))
    b.wait("sync", "dSp0", b.ctr["dSp0"])
    b.wait("sync", "dSp1", b.ctr["dSp1"])
    b.wait("tensor", "vV", nvq)
    for h in range(NH):
        b.need("sync", "kvis")
        nkv = b.dma("sync", "dKeep", kvis[:], kbuf[h * 128:(h + 1) * 128, 1:513])
        b.wait("tensor", "dKeep", nkv)
        npi = b.op("tensor", "matmul", R1[:], qbar_bf[:, h:h + 1], kvis[:],
                   start=(h == 0), stop=(h == NH - 1), inc=("pP", 1))
        b.released("kvis", "pP", npi)
    b.wait("vector", "pP", b.ctr["pP"])
    b.wait("vector", "dFlag", 16)
    nvi = b.op("vector", "tensor_scalar", imp_sb[:], R1[:], flag_sb[0:1, 0:1],
               None, ALU.mult, inc=("vV", 1))
    b.wait("sync", "vV", nvi)
    b.dma("sync", "dImp", imp_in[:, :], imp_sb[:])
    b.wait("gpsimd", "dImp", b.ctr["dImp"])
    b.op("gpsimd", "collective_compute", "AllReduce", ALU.add,
         replica_groups=[[0, 1, 2, 3, 4, 5, 6, 7]],
         ins=[imp_in.ap().opt()], outs=[imp_out.ap().opt()], inc=("cC", 1))

    if sub == "qki":
        return _cut1()
    v_pass(vwT)
    if sub == "qkiv":
        return _cut1()

    b.wait("sync", "cC", 1)
    b.wait("sync", "vV", 1)
    b.dma("sync", "dKeep", imp2_sb[:], imp_out[:, :])
    b.dma("sync", "dKeep", impshift[1:128, 0:1],
          imp_out[0, 0:127].rearrange("(r p) -> p r", p=127),
          allow_slow_non_contiguous=True)
    b.dma("sync", "dKeep", impshift[:, 1:4],
          imp_out[0, 127:511].rearrange("(r p) -> p r", p=128),
          allow_slow_non_contiguous=True)
    nks = b.dma("sync", "dKeep", impshift[0:1, 4:5],
                imp_out[0, 511:512].rearrange("(r p) -> p r", p=1),
                allow_slow_non_contiguous=True)
    b.wait("tensor", "dKeep", nks)
    b.need("tensor", "PRJ0")
    nbm = b.op("tensor", "matmul", PRJ[0][:], onesrowA[:], imp2_sb[:],
               start=True, stop=True, inc=("pP", 1))
    b.wait("scalar", "pP", nbm)
    nbs = b.op("scalar", "activation", Bsb[:], PRJ[0][:], AF.Copy, inc=("aA", 1))
    b.wait("vector", "aA", nbs)
    b.wait("vector", "dKeep", nks)
    b.wait("vector", "vV", nvi)
    for kc in range(4):
        b.opw("vector", "tensor_scalar", cmpscr[:], Bsb[:],
              impshift[:, kc:kc + 1], None, ALU.is_gt)
        b.opw("vector", "reduce_sum", rankc[:], cmpscr[:], AXX)
        b.op("vector", "tensor_scalar", keepf32[:, kc:kc + 1], rankc[:],
             255.5, None, ALU.is_lt)
    b.op("vector", "memset", keepf32[:, 4:6], 1.0)
    b.opw("vector", "tensor_scalar", cmpscr[0:1, :], Bsb[0:1, :],
          impshift[0:1, 4:5], None, ALU.is_gt)
    b.opw("vector", "reduce_sum", r512[:], cmpscr[0:1, :], AXX)
    b.op("vector", "tensor_scalar", keepf32[0:1, 4:5], r512[:],
         255.5, None, ALU.is_lt)
    b.opw("vector", "memset", keepf32[0:1, 0:1], 1.0)
    b.op("vector", "tensor_copy", keepfull[:], keepf32[:], inc=("vV", 1))
    b.wait("sync", "dSp0", b.ctr["dSp0"])
    b.wait("sync", "dSp1", b.ctr["dSp1"])
    for i6 in range(5):
        b.wait("sync", f"dFeat{i6}", b.ctr[f"dFeat{i6}"])
    nc.sbuf_base, nc.psum_base = sbP, psP
    b.run_block()
    if _cut(1):
        return nc

    b.start_block_waits()
    masks_sb = A("masks_sb", [128, 4 * 512], BF16)
    qh = [A("qh0", [128, S], BF16), A("qh1", [128, S], BF16)]
    Kh = [A("Kh0", [128, S], BF16), A("Kh1", [128, S], BF16)]
    Vh = [A("Vh0", [128, 32 * D], BF16), A("Vh1", [128, 32 * D], BF16)]
    pt = [A(f"pt{i}", [128, 512], BF16) for i in range(4)]
    rdb = [A("rdb0", [128, 512], F32), A("rdb1", [128, 512], F32)]
    attout = [A("attout0", [128, 512], BF16), A("attout1", [128, 512], BF16)]
    nm1 = A("nm1", [1, 1], F32)
    lnm = A("lnm", [1, 1], F32)
    ch = A("ch", [1, 1], F32)
    negc1 = A("negc1", [1, 1], F32)
    negcb = [A("negcb0", [128, 1], F32), A("negcb1", [128, 1], F32)]
    rcp = [A("rcp0", [1, 512], F32), A("rcp1", [1, 512], F32)]
    onesrow2 = A("onesrow2", [1, 128], F32)
    wnat = [A("wnat0", [128, HID], F32), A("wnat1", [128, HID], F32)]
    xfull = A("xfull", [128, HID], F32)
    sfull = A("sfull", [128, HID], F32)
    wqn = [A("wqn0", [128, HID], BF16), A("wqn1", [128, HID], BF16)]
    absm = A("absm", [128, 32], F32)
    sclq = A("sclq", [128, 32], F32)
    rinv = A("rinv", [128, 32], F32)
    gcol = A("gcol", [128, 1], F32)
    Spsum = [P(f"S{i}_", [128, 512], F32) for i in range(4)]
    CPV = [P("CPV0", [128, 512], F32), P("CPV1", [128, 512], F32)]
    DBC = P("DBC", [128, 512], F32)
    R1a = P("R1a", [1, 512], F32)
    NEG = DBC

    b.op("vector", "memset", onesrow2[:], 1.0, inc=("vV", 1))
    nmask = b.dma("sync", "dKeep", masks_sb[:].rearrange("p (o c) -> p o c", o=4),
                  masks.ap().rearrange("o p c -> p o c"))
    b.wait("vector", "dKeep", nmask)
    b.opw("vector", "memset", gcol[:], 1.0)
    b.op("vector", "tensor_scalar", gcol[:], gcol[:], C52, -C52, ALU.add,
         ALU.add, inc=("vV", 1))
    nwl = [None] * 8

    def load_wnat(ot):
        if ot < 8:
            b.need("sync", f"wnat{ot % 2}")
            nwl[ot] = b.dma("sync", f"dWn{ot % 2}", wnat[ot % 2][:],
                            fw_nat[ot * 128:(ot + 1) * 128, :])

    def quant_steps():
        load_wnat(0)
        load_wnat(1)
        for ot in range(8):
            par = ot % 2
            b.wait("vector", f"dWn{par}", nwl[ot])
            b.opw("vector", "tensor_reduce", absm[:],
                  wnat[par][:].rearrange("p (g c) -> p g c", c=128), AXX,
                  ALU.max, apply_absolute_value=True)
            b.opw("vector", "tensor_scalar", sclq[:], absm[:], 1.0 / 7.0, None,
                  ALU.mult)
            b.opw("vector", "tensor_scalar", absm[:], absm[:], 1e-30, None,
                  ALU.max)
            b.opw("vector", "reciprocal", rinv[:], absm[:])
            b.opw("vector", "tensor_scalar", rinv[:], rinv[:], 7.0, None,
                  ALU.mult)
            yield
            ngx = None
            for gb in range(4):
                for gi in range(8):
                    g = gb * 8 + gi
                    ngx = b.op("vector", "tensor_scalar",
                               xfull[:, g * 128:(g + 1) * 128],
                               wnat[par][:, g * 128:(g + 1) * 128],
                               rinv[:, g:g + 1], None, ALU.mult,
                               inc=("vV", 1) if g == 31 else None)
                yield
            b.wait("vector", "vV", ngx)
            nrel = b.op("vector", "tensor_scalar", xfull[:], xfull[:], C23,
                        -C23, ALU.add, ALU.add, inc=("vV", 1))
            b.released(f"wnat{par}", "vV", nrel)
            load_wnat(ot + 2)
            b.wait("vector", "vV", nrel)
            b.opw("vector", "tensor_scalar", sfull[:], xfull[:], C52, -C52,
                  ALU.add, ALU.add)
            b.opw("vector", "tensor_tensor", sfull[:], sfull[:], xfull[:],
                  ALU.subtract)
            b.opw("vector", "scalar_tensor_tensor", xfull[:], sfull[:],
                  gcol[:], xfull[:], ALU.mult, ALU.add)
            yield
            b.need("vector", f"wqn{par}")
            nq2 = None
            for gb in range(4):
                for gi in range(8):
                    g = gb * 8 + gi
                    nq2 = b.op("vector", "tensor_scalar",
                               wqn[par][:, g * 128:(g + 1) * 128],
                               xfull[:, g * 128:(g + 1) * 128],
                               sclq[:, g:g + 1], None, ALU.mult,
                               inc=("vV", 1) if g == 31 else None)
                yield
            b.wait("sync", "vV", nq2)
            nst = b.dma("sync", f"dWqn{par}",
                        wq_nat[ot * 128:(ot + 1) * 128, :], wqn[par][:])
            b.released(f"wqn{par}", f"dWqn{par}", nst)
            yield

    qgen = quant_steps()

    def att_loads(h):
        hp = h % 2
        b.need("sync", f"qh{hp}")
        nq = b.dma("sync", f"dQH{hp}", qh[hp][:], qbuf[h * 128:(h + 1) * 128, :])
        b.need("sync", f"Kh{hp}")
        nk = b.dma("sync", f"dKH{hp}", Kh[hp][:], kbuf[h * 128:(h + 1) * 128, :])
        b.need("sync", f"Vh{hp}")
        nv_ = b.dma("sync", f"dVH{hp}",
                    Vh[hp][:].rearrange("p (o kc d) -> p o kc d", o=1, kc=32),
                    vtok.ap()[h:h + 1, :, :, :].rearrange("o p kc d -> p o kc d"))
        return nq, nk, nv_

    pend = {0: att_loads(0), 1: None}
    pending_tail = []

    def emit_tail():
        while pending_tail:
            h_, j_, nrc_ = pending_tail.pop(0)
            jp_ = j_ % 2
            b.need("tensor", "DBC")
            b.wait("tensor", "vV", nrc_)
            nbc = b.op("tensor", "matmul", DBC[:], onesrow2[:], rcp[jp_][:],
                       start=True, stop=True, inc=("pP", 1))
            b.released(f"rcp{jp_}", "pP", nbc)
            b.wait("scalar", "pP", nbc)
            b.need("scalar", f"rdb{jp_}")
            nrd = b.op("scalar", "activation", rdb[jp_][:], DBC[:], AF.Copy,
                       inc=("aA", 1))
            b.released("DBC", "aA", nrd)
            b.wait("vector", "aA", nrd)
            b.need("vector", f"attout{jp_}")
            nao = b.op("vector", "tensor_tensor", attout[jp_][:], CPV[jp_][:],
                       rdb[jp_][:], ALU.mult, inc=("vV", 1))
            b.released(f"rdb{jp_}", "vV", nao)
            b.released(f"CPV{jp_}", "vV", nao)
            b.wait("sync", "vV", nao)
            nst = b.dma("sync", f"dAtt{jp_}",
                        att_mine[h_ * 128:(h_ + 1) * 128,
                                 j_ * 512:(j_ + 1) * 512],
                        attout[jp_][:])
            b.released(f"attout{jp_}", f"dAtt{jp_}", nst)

    for h in range(NH):
        hp = h % 2
        if h + 1 < NH:
            pend[(h + 1) % 2] = att_loads(h + 1)
        nq, nk, nv_ = pend[hp]
        b.opw("vector", "tensor_tensor", nm1[:], maxq2[:, h:h + 1],
              maxk2[:, h:h + 1], ALU.mult)
        nvn = b.op("vector", "tensor_scalar", nm1[:], nm1[:], 1e-30, None,
                   ALU.max, inc=("vV", 1))
        b.wait("scalar", "vV", nvn)
        b.opw("scalar", "activation", lnm[:], nm1[:], AF.Ln)
        b.opw("scalar", "activation", ch[:], lnm[:], AF.Exp, scale=0.5)
        nac = b.op("scalar", "mul", negc1[:], ch[:], -SCALE, inc=("aA", 1))
        b.wait("tensor", "aA", nac)
        b.need("tensor", "DBC")
        nnp = b.op("tensor", "matmul", NEG[:, 0:1], onesrow2[:], negc1[:],
                   start=True, stop=True, inc=("pP", 1))
        b.wait("scalar", "pP", nnp)
        nnc = b.op("scalar", "activation", negcb[hp][:], NEG[:, 0:1], AF.Copy,
                   inc=("aA", 1))
        b.wait("scalar", "aA", nnc)
        b.released("DBC", "aA", nnc)
        b.wait("vector", f"dVH{hp}", nv_)
        nvz = None
        for kc in range(5):
            nvz = b.op("vector", "tensor_scalar",
                       Vh[hp][:, kc * D:(kc + 1) * D],
                       Vh[hp][:, kc * D:(kc + 1) * D],
                       keepf32[:, kc:kc + 1], None, ALU.mult, inc=("vV", 1))
        b.wait("tensor", "vV", nvz)
        b.wait("tensor", f"dQH{hp}", nq)
        b.wait("tensor", f"dKH{hp}", nk)
        for j in range(8):
            kcn = 4 * j + 4
            jp = j % 2

            def emit_S(kc, hp=hp, j=j):
                sp_ = kc % 4
                b.need("tensor", f"S{sp_}")
                return b.op("tensor", "matmul", Spsum[sp_][:],
                            Kh[hp][:, kc * 128:(kc + 1) * 128],
                            qh[hp][:, j * 512:(j + 1) * 512],
                            start=True, stop=True, inc=("pP", 1))
            ns_of = {}
            for kc in range(min(4, kcn)):
                ns_of[kc] = emit_S(kc)
            emit_tail()
            b.need("tensor", "R1a")
            b.need("tensor", f"CPV{jp}")
            for kc in range(kcn):
                sp_ = kc % 4
                ns = ns_of.pop(kc)
                b.wait("scalar", "pP", ns)
                if j == 0 and kc == 0:
                    b.wait("scalar", "aA", nnc)
                b.need("scalar", f"pt{sp_}")
                nex = b.op("scalar", "activation", pt[sp_][:], Spsum[sp_][:],
                           AF.Exp, bias=negcb[hp][:], scale=SCALE,
                           inc=("aA", 1))
                b.released(f"S{sp_}", "aA", nex)
                gate = ("aA", nex)
                if kc >= 4 * j:
                    o = kc - 4 * j
                    b.wait("vector", "aA", nex)
                    nmk = b.op("vector", "tensor_tensor", pt[sp_][:], pt[sp_][:],
                               masks_sb[:, o * 512:(o + 1) * 512], ALU.mult,
                               inc=("vV", 1))
                    gate = ("vV", nmk)
                b.wait("tensor", gate[0], gate[1])
                kk = min(kc, 5)
                b.op("tensor", "matmul", R1a[:], keepfull[:, kk:kk + 1],
                     pt[sp_][:], start=(kc == 0), stop=(kc == kcn - 1))
                npv = b.op("tensor", "matmul", CPV[jp][:],
                           Vh[hp][:, kc * D:(kc + 1) * D], pt[sp_][:],
                           start=(kc == 0), stop=(kc == kcn - 1), inc=("pP", 1))
                b.released(f"pt{sp_}", "pP", npv)
                if kc + 4 < kcn:
                    ns_of[kc + 4] = emit_S(kc + 4)
            b.wait("vector", "pP", b.ctr["pP"])
            b.need("vector", f"rcp{jp}")
            nrc = b.op("vector", "reciprocal", rcp[jp][:], R1a[:], inc=("vV", 1))
            b.released("R1a", "vV", nrc)
            pending_tail.append((h, j, nrc))
            next(qgen, None)
        b.released(f"qh{hp}", "pP", b.ctr["pP"])
        b.released(f"Kh{hp}", "pP", b.ctr["pP"])
        b.released(f"Vh{hp}", "pP", b.ctr["pP"])
    emit_tail()
    for _ in qgen:
        pass
    b.wait("gpsimd", "dAtt0", b.ctr["dAtt0"])
    b.wait("gpsimd", "dAtt1", b.ctr["dAtt1"])
    b.op("gpsimd", "collective_compute", "AllGather", ALU.bypass,
         replica_groups=[[0, 1, 2, 3], [4, 5, 6, 7]],
         ins=[att_mine.ap().opt()], outs=[att_full.ap().opt()], inc=("cC", 1))
    nc.sbuf_base, nc.psum_base = sbP, psP
    b.run_block()
    if _cut(2):
        return nc

    b.start_block_waits()
    wqT_sb = A("wqT_sb", [128, 32 * 1024], BF16)
    sbP3, psP3 = nc.sbuf_base, nc.psum_base
    b.wait("sync", "dWqn0", b.ctr["dWqn0"])
    b.wait("sync", "dWqn1", b.ctr["dWqn1"])
    for ic in range(32):
        b.dma("sync", "dWqT",
              wqT_sb[:, ic * 1024: (ic + 1) * 1024],
              wq_nat[:, ic * 128:(ic + 1) * 128], transpose=True)
    nc.sbuf_base, nc.psum_base = sbP3, psP3
    b.run_block()
    if _cut(3):
        return nc

    b.start_block_waits()
    osb = A("osb", [128, 32 * 1024], BF16)
    att_c = [A("att_c0", [128, 32 * 256], BF16), A("att_c1", [128, 32 * 256], BF16)]
    xnc3 = [A("xnc30", [128, 32 * 256], BF16), A("xnc31", [128, 32 * 256], BF16)]
    xresb = A("xresb", [128, 1024], F32)
    outb = [A("outb0", [128, 512], F32), A("outb1", [128, 512], F32)]
    biasv = A("biasv", [1, FS], F32)
    biasB = A("biasB", [128, FS], BF16)
    onesrow3 = A("onesrow3", [1, 128], F32)
    P3 = [P("P3A", [128, 512], F32), P("P3B", [128, 512], F32)]

    nv3 = b.op("vector", "memset", onesrow3[:], 1.0, inc=("vV", 1))
    b.dma("sync", "dBias", biasv[:], fbias[:, :])
    for cc in range(4):
        b.dma("gpsimd", "dOsb",
              osb[:].rearrange("p (c n) -> p c n", c=32)[:, cc * 8:(cc + 1) * 8, :],
              owT.ap().rearrange("(c p) n -> p c n", p=128)[:, cc * 8:(cc + 1) * 8, :])
    b.wait("sync", "cC", 2)
    b.wait("tensor", "dBias", b.ctr["dBias"])
    b.wait("tensor", "vV", nv3)
    for fo in range(2):
        nb1 = b.op("tensor", "matmul", P3[fo][:], onesrow3[:],
                   biasv[0:1, fo * 512:(fo + 1) * 512], start=True, stop=True,
                   inc=("pP", 1))
        b.wait("scalar", "pP", nb1)
        nb2 = b.op("scalar", "activation", biasB[:, fo * 512:(fo + 1) * 512],
                   P3[fo][:], AF.Copy, inc=("aA", 1))
        b.released(f"P3{fo}", "aA", nb2)
    b.wait("tensor", "dWqT", b.ctr["dWqT"])
    b.wait("tensor", "dOsb", b.ctr["dOsb"])

    def p3_loads(tc):
        tp = tc % 2
        t0 = tc * 256
        b.need("sync", f"att_c{tp}")
        nac_ = None
        for cc in range(2):
            nac_ = b.dma("sync", f"dAC{tp}",
                         att_c[tp][:].rearrange("p (c n) -> p c n", c=32)
                         [:, cc * 16:(cc + 1) * 16, :],
                         att_full[:, t0:t0 + 256]
                         .rearrange("(c p) n -> p c n", p=128)
                         [:, cc * 16:(cc + 1) * 16, :])
        b.need("sync", f"xnc3{tp}")
        nxc3 = None
        for cc in range(2):
            nxc3 = b.dma("sync", f"dXn3{tp}",
                         xnc3[tp][:].rearrange("p (c n) -> p c n", c=32)
                         [:, cc * 16:(cc + 1) * 16, :],
                         xn_feat[:, t0:t0 + 256]
                         .rearrange("(c p) n -> p c n", p=128)
                         [:, cc * 16:(cc + 1) * 16, :])
        return nac_, nxc3

    pend3 = {0: p3_loads(0), 1: None}
    for tc in range(16):
        tp = tc % 2
        t0 = tc * 256
        if tc + 1 < 16:
            pend3[(tc + 1) % 2] = p3_loads(tc + 1)
        nac_, nxc3 = pend3[tp]
        b.wait("tensor", f"dAC{tp}", nac_)
        b.wait("tensor", f"dXn3{tp}", nxc3)
        for tt in range(2):
            trow = t0 + tt * 128
            b.need("sync", "xres")
            nxr = b.dma("sync", "dXr0", xresb[:],
                        xres_i[trow:trow + 128, :])
            b.wait("vector", "dXr0", nxr)
            for fo in range(2):
                pp = (tc * 4 + tt * 2 + fo) % 2
                op_ = pp
                b.need("tensor", f"P3{pp}")
                for hc in range(32):
                    b.op("tensor", "matmul", P3[pp][:],
                         att_c[tp][:, hc * 256 + tt * 128: hc * 256 + tt * 128 + 128],
                         osb[:, hc * 1024 + fo * 512: hc * 1024 + fo * 512 + 512],
                         start=(hc == 0), stop=False)
                for ic in range(32):
                    b.op("tensor", "matmul", P3[pp][:],
                         xnc3[tp][:, ic * 256 + tt * 128: ic * 256 + tt * 128 + 128],
                         wqT_sb[:, ic * 1024 + fo * 512: ic * 1024 + fo * 512 + 512],
                         start=False, stop=(ic == 31),
                         inc=("pP", 1) if ic == 31 else None)
                npf = b.ctr["pP"]
                b.wait("vector", "pP", npf)
                b.need("vector", f"outb{op_}")
                b.opw("vector", "tensor_tensor", outb[op_][:], P3[pp][:],
                      biasB[:, fo * 512:(fo + 1) * 512], ALU.add)
                nvo = b.op("vector", "tensor_tensor", outb[op_][:], outb[op_][:],
                           xresb[:, fo * 512:(fo + 1) * 512], ALU.add,
                           inc=("vV", 1))
                b.released(f"P3{pp}", "vV", nvo)
                if fo == 1:
                    b.released("xres", "vV", nvo)
                b.wait("sync", "vV", nvo)
                nyw = b.dma("sync", f"dY{op_}",
                            y[trow:trow + 128, fo * 512:(fo + 1) * 512],
                            outb[op_][:])
                b.released(f"outb{op_}", f"dY{op_}", nyw)
        b.released(f"att_c{tp}", "pP", b.ctr["pP"])
        b.released(f"xnc3{tp}", "pP", b.ctr["pP"])
    b.wait("sync", "dY0", b.ctr["dY0"])
    b.wait("sync", "dY1", b.ctr["dY1"])
    b.run_block()

    nc.finalize()
    return nc


_NC_CACHE = []


def _host_tables():
    inv = (10000.0 ** (-(np.arange(64, dtype=np.float64) / 64.0)))
    t = np.arange(S, dtype=np.float64)
    ang = t[None, :] * inv[:, None]
    ang = np.concatenate([ang, ang], axis=0)
    cosTv = np.cos(ang).astype(np.float32)
    sinTv = np.sin(ang).astype(np.float32)
    p = np.arange(128)[:, None]
    c = np.arange(512)[None, :]
    m = np.stack([(p + 128 * o <= c) for o in range(4)], 0)
    masksv = m.astype(ml_dtypes.bfloat16)
    return cosTv, sinTv, masksv


def kernel(L0_attn_hidden_states, L0_attn_q_w, L0_attn_k_w, L0_attn_v_w,
           L0_attn_o_w, L0_ffn_weight, L0_ffn_bias):
    x = np.asarray(L0_attn_hidden_states)
    qwTv = np.ascontiguousarray(np.asarray(L0_attn_q_w).T)
    kwTv = np.ascontiguousarray(np.asarray(L0_attn_k_w).T)
    vwTv = np.ascontiguousarray(np.asarray(L0_attn_v_w).T)
    owTv = np.ascontiguousarray(np.asarray(L0_attn_o_w).T)
    fw = np.asarray(L0_ffn_weight)
    fb = np.asarray(L0_ffn_bias)
    cosTv, sinTv, masksv = _host_tables()

    if not _NC_CACHE:
        _NC_CACHE.append(build())
    nc = _NC_CACHE[0]

    in_maps = []
    for c in range(8):
        bidx, r = c // 4, c % 4
        fs = slice(r * FS, (r + 1) * FS)
        in_maps.append({
            "x_b": np.ascontiguousarray(x[bidx]),
            "xres_i": np.ascontiguousarray(x[bidx][:, fs]),
            "qwT": np.ascontiguousarray(qwTv[:, fs]),
            "kwT": np.ascontiguousarray(kwTv[:, fs]),
            "vwT": np.ascontiguousarray(vwTv[:, fs]),
            "owT": np.ascontiguousarray(owTv[:, fs]),
            "fw_nat": np.ascontiguousarray(fw[fs, :]),
            "fbias": np.ascontiguousarray(fb[fs][None, :]),
            "cosT": cosTv, "sinT": sinTv, "masks": masksv,
            "flag": np.array([[1.0 if bidx == 0 else 0.0]], np.float32),
        })
    res = run_bass_kernel_spmd(nc, in_maps, core_ids=list(range(8)))
    out = np.empty((2, S, HID), np.float32)
    for c in range(8):
        bidx, r = c // 4, c % 4
        out[bidx][:, r * FS:(r + 1) * FS] = np.asarray(res.results[c]["y"])
    return out
